# version 13
# baseline (speedup 1.0000x reference)
import numpy as np
import ml_dtypes
import concourse.bass as bass
import concourse.mybir as mybir
from concourse.bass_utils import run_bass_kernel_spmd

F32 = mybir.dt.float32
BF16 = mybir.dt.bfloat16
I32 = mybir.dt.int32
AF = mybir.ActivationFunctionType
OP = mybir.AluOpType

D = 2048
I_DIM = 8192
NH = 16
HD = 128
FC = D // 128
NISB = 32
ISB_I = I_DIM // NISB
EPS = 1e-5
STAGE = 3


class Sem:
    __slots__ = ("name", "n", "h", "base")

    def __init__(self, name):
        self.name = name
        self.n = 0
        self.base = 0
        self.h = None


class Rec:
    def __init__(self):
        self.ops = {k: [] for k in "TAVPS"}

    def emit(self, ek, fn, waits=(), inc=None, amt=1):
        ws = tuple((s, v) for (s, v) in waits)
        if inc is not None:
            inc.n += amt
        def run(eng, ws=ws, fn=fn, inc=inc, amt=amt):
            for s, v in ws:
                if v > 0:
                    eng.wait_ge(s.h, v)
            ins = fn(eng) if fn is not None else None
            if inc is not None:
                ins.then_inc(inc.h, amt)
        self.ops[ek].append(run)


def build(T_CORE):
    TH = T_CORE // 2
    NTG = TH // 512
    KEYS = 4 * T_CORE
    NKT = KEYS // 128
    QH = T_CORE // 2
    assert NTG >= 1 and QH <= 1024

    nc = bass.Bass("TRN2", target_bir_lowering=False, debug=False)

    x_in = nc.dram_tensor("x_fm", [D, T_CORE], F32, kind="ExternalInput")
    gate_w = nc.dram_tensor("gate_w", [I_DIM, D], I32, kind="ExternalInput")
    up_w = nc.dram_tensor("up_w", [I_DIM, D], I32, kind="ExternalInput")
    down_w = nc.dram_tensor("down_w", [D, I_DIM], I32, kind="ExternalInput")
    wq = nc.dram_tensor("wq", [D, D], F32, kind="ExternalInput")
    wk = nc.dram_tensor("wk", [D, HD], F32, kind="ExternalInput")
    wv = nc.dram_tensor("wv", [D, HD], F32, kind="ExternalInput")
    wo = nc.dram_tensor("wo", [D, D], F32, kind="ExternalInput")
    gsc_in = nc.dram_tensor("gsc", [128, 64], F32, kind="ExternalInput")
    usc_in = nc.dram_tensor("usc", [128, 64], F32, kind="ExternalInput")
    dsc_in = nc.dram_tensor("dsc", [128, 16], F32, kind="ExternalInput")
    bq_in = nc.dram_tensor("bq", [128, 16], F32, kind="ExternalInput")
    bk_in = nc.dram_tensor("bk", [128, 1], F32, kind="ExternalInput")
    bv_in = nc.dram_tensor("bv", [128, 1], F32, kind="ExternalInput")
    bo_in = nc.dram_tensor("bo", [128, 16], F32, kind="ExternalInput")
    cosq_in = nc.dram_tensor("cosq", [128, T_CORE], F32, kind="ExternalInput")
    sinq_in = nc.dram_tensor("sinq", [128, T_CORE], F32, kind="ExternalInput")
    cosk_in = nc.dram_tensor("cosk", [128, T_CORE], F32, kind="ExternalInput")
    sink_in = nc.dram_tensor("sink", [128, T_CORE], F32, kind="ExternalInput")
    ident_in = nc.dram_tensor("ident", [128, 128], BF16, kind="ExternalInput")

    out_fm = nc.dram_tensor("out_fm", [D, T_CORE], F32, kind="ExternalOutput")

    xs = nc.dram_tensor("xs", [D, T_CORE], F32)
    scr_g = nc.dram_tensor("scr_g", [I_DIM, D], BF16)
    scr_u = nc.dram_tensor("scr_u", [I_DIM, D], BF16)
    scr_d = nc.dram_tensor("scr_d", [NISB * D, ISB_I], BF16)
    scr2_g = nc.dram_tensor("scr2_g", [NISB * 128, 16 * ISB_I], BF16)
    scr2_u = nc.dram_tensor("scr2_u", [NISB * 128, 16 * ISB_I], BF16)
    scr2_d = nc.dram_tensor("scr2_d", [NISB * 128, 2 * D], BF16)
    cc_ik = nc.dram_tensor("cc_ik", [128, T_CORE], BF16)
    cc_ok = nc.dram_tensor("cc_ok", [512, T_CORE], BF16)
    cc_iv = nc.dram_tensor("cc_iv", [T_CORE, 128], BF16)
    cc_ov = nc.dram_tensor("cc_ov", [4 * T_CORE, 128], BF16)

    rec = Rec()
    sw, sg, st, scc = Sem("sw"), Sem("sg"), Sem("st"), Sem("scc")
    spc, swo, std, swd = Sem("spc"), Sem("swo"), Sem("std"), Sem("swd")
    cT, cA, cV, cP = Sem("cT"), Sem("cA"), Sem("cV"), Sem("cP")
    sems = [sw, sg, st, scc, spc, swo, std, swd, cT, cA, cV, cP]

    from contextlib import ExitStack
    with ExitStack() as ctx:
        xnt = ctx.enter_context(nc.sbuf_tensor("xnt", [128, FC * T_CORE], BF16))
        yacc = ctx.enter_context(nc.sbuf_tensor("yacc", [128, 16 * TH], F32))
        r3 = ctx.enter_context(nc.sbuf_tensor("r3", [128, 28672], BF16))
        r4 = ctx.enter_context(nc.sbuf_tensor("r4", [128, 2048], F32))
        r5 = ctx.enter_context(nc.sbuf_tensor("r5", [128, 1280], F32))
        r6 = ctx.enter_context(nc.sbuf_tensor("r6", [128, 4096], BF16))
        onesb = ctx.enter_context(nc.sbuf_tensor("onesb", [128, 2], BF16))
        onesf = ctx.enter_context(nc.sbuf_tensor("onesf", [128, 1], F32))
        onesrow = ctx.enter_context(nc.sbuf_tensor("onesrow", [1, 128], F32))
        ident = ctx.enter_context(nc.sbuf_tensor("ident_sb", [128, 128], BF16))
        pstg = ctx.enter_context(nc.sbuf_tensor("pstg", [128, 1024], BF16))
        pA = ctx.enter_context(nc.psum_tensor("pA", [128, 1024], F32))
        pB = ctx.enter_context(nc.psum_tensor("pB", [128, 1024], F32))
        pC = ctx.enter_context(nc.psum_tensor("pC", [128, 2048], F32))
        for s in sems:
            s.h = ctx.enter_context(nc.semaphore(s.name))
        block = ctx.enter_context(nc.Block())

        xnt_f = xnt.bitcast(F32)
        yacc_b = yacc.bitcast(BF16)
        r3f = r3.bitcast(F32)
        pAb = pA.bitcast(BF16)
        pBb = pB.bitcast(BF16)

        GSC, USC, DSC, DSCH = 0, 64, 128, 144
        BQ, BO, BK, BV = 160, 176, 192, 193
        EPSs, ONESF_UNUSED, MHN, R2M, KAP, TMPS = 194, 195, 196, 212, 228, 229
        LNM, LNV = 256, 768

        WG, WU, WD, HDN = 0, 8192, 16384, 24576
        KK, VV, WOB, PRING, CTXS = 8192, 0, 16384, 20480, 24576
        TCOSQ, TSINQ, TCOSK, TSINK = 0, 2048, 4096, 6144
        QB_F, SWP_F = 10240, 12288
        WQB = 16384
        VSTG = 20480

        def dma(ek, out, in_, sem, waits=(), ng=True, **kw):
            if ng:
                sem.base = sem.n
            waits = list(waits) + [(sem, sem.base)]
            rec.emit(ek, lambda e, o=out, i=in_, kw=kw: e.dma_start(out=o, in_=i, **kw),
                     waits=waits, inc=sem, amt=16)

        def init_phase():
            dma("P", r5[:, GSC:GSC + 64], gsc_in[:, :], sg)
            dma("P", r5[:, USC:USC + 64], usc_in[:, :], sg, ng=False)
            dma("P", r5[:, DSC:DSC + 16], dsc_in[:, :], sg, ng=False)
            dma("P", r5[:, BQ:BQ + 16], bq_in[:, :], sg, ng=False)
            dma("P", r5[:, BK:BK + 1], bk_in[:, :], sg, ng=False)
            dma("P", r5[:, BV:BV + 1], bv_in[:, :], sg, ng=False)
            dma("P", r5[:, BO:BO + 16], bo_in[:, :], sg, ng=False)
            dma("P", ident[:, :], ident_in[:, :], sg, ng=False)
            rec.emit("P", lambda e: e.memset(onesb[:, :], 1.0), inc=cP)
            rec.emit("P", lambda e: e.memset(onesf[:, :], 1.0), inc=cP)
            rec.emit("P", lambda e: e.memset(onesrow[:, :], 1.0), inc=cP)
            rec.emit("P", lambda e: e.memset(r5[:, EPSs:EPSs + 1], EPS), inc=cP)
            rec.emit("V", lambda e: e.tensor_scalar_mul(
                r5[:, DSCH:DSCH + 16], r5[:, DSC:DSC + 16], 0.5),
                waits=[(sg, sg.n)], inc=cV)

        cast_mark = {}

        def prologue_casts():
            for isb in range(NISB):
                r0 = isb * ISB_I
                dma("P", scr_g[r0:r0 + ISB_I, :], gate_w[r0:r0 + ISB_I, :], spc)
                dma("P", scr_u[r0:r0 + ISB_I, :], up_w[r0:r0 + ISB_I, :], spc, ng=False)
                dma("P", scr_d[isb * D:(isb + 1) * D, :], down_w[:, r0:r0 + ISB_I],
                    spc, ng=False)
                cast_mark[isb] = spc.n

        def ln_half(src, t0, dep_waits=(), stage=None, psA=None, psB=None,
                    fc_waits=None, bf16_stage=False):
            ntg = TH // 512
            e_cV, e_cT, e_st = cV.n, cT.n, st.n
            xbf = [r6[:, 0:TH], r6[:, 1024:1024 + TH]]
            sqb = [r6[:, 2048:2048 + TH], r6[:, 3072:3072 + TH]]
            xst = stage if stage is not None else [r4[:, 0:TH], r4[:, 1024:1024 + TH]]
            if psA is None:
                psA, psB = pA, pB
            mm_done = {}
            for fc in range(FC):
                b = fc % 2
                w = list(dep_waits) if fc < 2 else []
                if fc_waits is not None:
                    w.append(fc_waits[fc])
                if fc < 2:
                    w.append((cV, e_cV))
                    if fc_waits is None:
                        w.append((st, e_st))
                if fc >= 2:
                    w.append((cT, mm_done[fc - 2]))
                if bf16_stage:
                    dma("P", xbf[b], src[fc * 128:(fc + 1) * 128, t0:t0 + TH], sg,
                        waits=w + [(cT, mm_done.get(fc - 2, e_cT))])
                    rec.emit("A", lambda e, b=b: e.activation(
                        out=sqb[b], in_=xbf[b], func=AF.Square),
                        waits=[(sg, sg.n), (cA, cA.n),
                               (cV, e_cV if fc < 2 else 0)], inc=cA)
                else:
                    dma("P", xst[b], src[fc * 128:(fc + 1) * 128, t0:t0 + TH], sg,
                        waits=w)
                    rec.emit("A", lambda e, b=b: e.activation(
                        out=xbf[b], in_=xst[b], func=AF.Copy),
                        waits=[(sg, sg.n), (cT, mm_done.get(fc - 2, e_cT)),
                               (cV, e_cV if fc < 2 else 0)], inc=cA)
                    rec.emit("A", lambda e, b=b: e.activation(
                        out=sqb[b], in_=xbf[b], func=AF.Square),
                        waits=[(cA, cA.n)], inc=cA)
                aw = cA.n
                for tg in range(ntg):
                    rec.emit("T", lambda e, b=b, tg=tg, fc=fc: e.matmul(
                        psA[0:1, tg * 512:tg * 512 + 512], onesb[:, 0:1],
                        xbf[b][:, tg * 512:tg * 512 + 512],
                        start=(fc == 0), stop=(fc == FC - 1)),
                        waits=[(cA, aw)] if tg == 0 else [])
                    rec.emit("T", lambda e, b=b, tg=tg, fc=fc: e.matmul(
                        psB[0:1, tg * 512:tg * 512 + 512], onesb[:, 0:1],
                        sqb[b][:, tg * 512:tg * 512 + 512],
                        start=(fc == 0), stop=(fc == FC - 1)),
                        inc=cT if (tg == ntg - 1) else None)
                mm_done[fc] = cT.n
            stat_done = cT.n
            bc_mark = {}
            for tg in range(ntg):
                s0 = tg * 512
                w0 = [(cT, stat_done)]
                if tg > 0:
                    w0 += [(cT, bc_mark[tg - 1]), (cV, cV.n)]
                rec.emit("V", lambda e, s0=s0: e.tensor_scalar_mul(
                    r5[0:1, LNM:LNM + 512], psA[0:1, s0:s0 + 512], 1.0 / D),
                    waits=w0, inc=cV)
                rec.emit("V", lambda e, s0=s0: e.tensor_scalar_mul(
                    r5[0:1, LNV:LNV + 512], psB[0:1, s0:s0 + 512], 1.0 / D), inc=cV)
                rec.emit("V", lambda e, s0=s0: e.tensor_tensor(
                    psA[0:1, s0:s0 + 512], r5[0:1, LNM:LNM + 512],
                    r5[0:1, LNM:LNM + 512], OP.mult), waits=[(cV, cV.n)], inc=cV)
                rec.emit("V", lambda e, s0=s0: e.tensor_tensor(
                    r5[0:1, LNV:LNV + 512], r5[0:1, LNV:LNV + 512],
                    psA[0:1, s0:s0 + 512], OP.subtract), waits=[(cV, cV.n)], inc=cV)
                vdone = cV.n
                rec.emit("T", lambda e, s0=s0: e.matmul(
                    psA[:, s0:s0 + 512], onesrow[0:1, 0:128], r5[0:1, LNM:LNM + 512],
                    start=True, stop=True), waits=[(cV, vdone)], inc=cT)
                mb_mark = cT.n
                rec.emit("A", lambda e: e.activation(
                    out=r5[0:1, LNM:LNM + 512], in_=r5[0:1, LNV:LNV + 512],
                    func=AF.Sqrt, bias=r5[0:1, EPSs:EPSs + 1]),
                    waits=[(cV, vdone), (cT, mb_mark)], inc=cA)
                rec.emit("V", lambda e: e.reciprocal(
                    r5[0:1, LNM:LNM + 512], r5[0:1, LNM:LNM + 512]),
                    waits=[(cA, cA.n)], inc=cV)
                rec.emit("T", lambda e, s0=s0: e.matmul(
                    psB[:, s0:s0 + 512], onesrow[0:1, 0:128], r5[0:1, LNM:LNM + 512],
                    start=True, stop=True), waits=[(cV, cV.n)], inc=cT)
                bc_mark[tg] = cT.n
            e2_cT = cT.n
            st_dep = {}
            pf = pstg.bitcast(F32)
            for fc in range(FC):
                b = fc % 2
                w = [(cV, st_dep.get(fc - 2, 0))]
                if fc < 2:
                    w.append((cT, bc_mark[ntg - 1]))
                if bf16_stage:
                    dma("P", xbf[b], src[fc * 128:(fc + 1) * 128, t0:t0 + TH], sg,
                        waits=w)
                    ld = sg.n
                    for tg in range(ntg):
                        s0 = tg * 512
                        rec.emit("V", lambda e, b=b, s0=s0: e.tensor_tensor(
                            pf[:, 0:512], xbf[b][:, s0:s0 + 512],
                            psA[:, s0:s0 + 512], OP.subtract),
                            waits=[(sg, ld), (cT, e2_cT), (cV, cV.n)], inc=cV)
                        rec.emit("V", lambda e, s0=s0, fc=fc: e.tensor_tensor(
                            xnt[:, fc * T_CORE + t0 + s0: fc * T_CORE + t0 + s0 + 512],
                            pf[:, 0:512], psB[:, s0:s0 + 512], OP.mult),
                            waits=[(cV, cV.n)], inc=cV)
                else:
                    dma("P", xst[b], src[fc * 128:(fc + 1) * 128, t0:t0 + TH], sg,
                        waits=w)
                    ld = sg.n
                    for tg in range(ntg):
                        s0 = tg * 512
                        rec.emit("V", lambda e, b=b, s0=s0: e.tensor_tensor(
                            xst[b][:, s0:s0 + 512], xst[b][:, s0:s0 + 512],
                            psA[:, s0:s0 + 512], OP.subtract),
                            waits=[(sg, ld), (cT, e2_cT)], inc=cV)
                        rec.emit("V", lambda e, b=b, s0=s0, fc=fc: e.tensor_tensor(
                            xnt[:, fc * T_CORE + t0 + s0: fc * T_CORE + t0 + s0 + 512],
                            xst[b][:, s0:s0 + 512], psB[:, s0:s0 + 512], OP.mult),
                            waits=[(cV, cV.n)], inc=cV)
                st_dep[fc] = cV.n
            return cV.n

        def ffn_half(app, half, xsrc, xdst, xn_mark, direct=False, fuse_ln=False,
                     fresh=True):
            t0 = half * TH
            ntg = TH // 512
            e_cT, e_cV, e_cA, e_st = cT.n, cV.n, cA.n, st.n
            wgb = [r3[:, WG:WG + 4096], r3[:, WG + 4096:WG + 8192]]
            wub = [r3[:, WU:WU + 4096], r3[:, WU + 4096:WU + 8192]]
            wdb = [r3[:, WD:WD + 4096], r3[:, WD + 4096:WD + 8192]]
            hdn = [r3[:, HDN:HDN + 2048], r3[:, HDN + 2048:HDN + 4096]]
            sgb = r6[:, 0:TH]
            rec.emit("P", lambda e: e.memset(yacc[:, :], 0.0),
                     waits=[(cA, cA.n), (cV, cV.n), (st, st.n)], inc=cP)
            yinit = cP.n
            mm_done = {}
            gu_mark = {}
            w_ready = {}
            wd_ready = {}
            yadd_mark = {}
            pd_free = {}
            stt_mark = {}
            silu_mark = {}
            scr2w = {}
            for isb in range(NISB):
                buf = isb % 2
                war = [(cT, gu_mark.get(isb - 2, e_cT))]
                wdw = [(cT, mm_done.get(isb - 2, e_cT))]
                if isb < 2 and fresh:
                    war += [(cV, e_cV), (cA, e_cA)]
                    wdw += [(cV, e_cV), (cA, e_cA)]
                if direct:
                    war += [(spc, cast_mark[isb]), (st, scr2w.get(isb - 2, e_st))]
                    wdw += [(spc, cast_mark[isb]), (st, scr2w.get(isb - 2, e_st))]
                    for hc in range(16):
                        dma("S", wgb[buf][:, hc * 256:hc * 256 + 256],
                            scr_g[isb * 256:isb * 256 + 256, hc * 128:hc * 128 + 128],
                            sw, waits=war if hc == 0 else [],
                            ng=(hc == 0), transpose=True)
                        dma("S", wub[buf][:, hc * 256:hc * 256 + 256],
                            scr_u[isb * 256:isb * 256 + 256, hc * 128:hc * 128 + 128],
                            sw, ng=False, transpose=True)
                    ldm = sw.n
                    for wb in (wgb, wub):
                        rec.emit("A", lambda e, wb=wb, buf=buf: e.activation(
                            out=wb[buf], in_=wb[buf], func=AF.Copy, bias=-128.0),
                            waits=[(sw, ldm)], inc=cA)
                    negm = cA.n
                    for ic2 in range(2):
                        dma("S", wdb[buf][:, ic2 * 2048:ic2 * 2048 + 2048],
                            scr_d[isb * D:(isb + 1) * D, ic2 * 128:ic2 * 128 + 128],
                            swd, waits=wdw if ic2 == 0 else [],
                            ng=(ic2 == 0), transpose=True)
                    ldmd = swd.n
                    rec.emit("A", lambda e, buf=buf: e.activation(
                        out=wdb[buf], in_=wdb[buf], func=AF.Copy, bias=-128.0),
                        waits=[(swd, ldmd)], inc=cA)
                    negmd = cA.n
                    dma("A", scr2_g[isb * 128:(isb + 1) * 128, :], wgb[buf], st,
                        waits=[(cA, negm)])
                    dma("A", scr2_u[isb * 128:(isb + 1) * 128, :], wub[buf], st,
                        ng=False)
                    dma("A", scr2_d[isb * 128:(isb + 1) * 128, :], wdb[buf], st,
                        ng=False, waits=[(cA, negmd)])
                    scr2w[isb] = st.n
                    w_ready[isb] = (cA, negm)
                    wd_ready[isb] = (cA, negmd)
                else:
                    dma("S", wgb[buf], scr2_g[isb * 128:(isb + 1) * 128, :], sw,
                        waits=war)
                    dma("S", wub[buf], scr2_u[isb * 128:(isb + 1) * 128, :], sw,
                        ng=False)
                    w_ready[isb] = (sw, sw.n)
                    dma("S", wdb[buf], scr2_d[isb * 128:(isb + 1) * 128, :], swd,
                        waits=wdw)
                    wd_ready[isb] = (swd, swd.n)
                for ic2 in range(2):
                    ic = isb * 2 + ic2
                    tw = [w_ready[isb], (cA, silu_mark.get(ic - 1, 0)),
                          (cV, xn_mark)]
                    first = True
                    for hc in range(16):
                        for tg in range(ntg):
                            s0 = tg * 512
                            rec.emit("T", lambda e, buf=buf, hc=hc, ic2=ic2, s0=s0: e.matmul(
                                pA[:, s0:s0 + 512],
                                wgb[buf][:, hc * 256 + ic2 * 128:hc * 256 + ic2 * 128 + 128],
                                xnt[:, hc * T_CORE + t0 + s0: hc * T_CORE + t0 + s0 + 512],
                                start=(hc == 0), stop=(hc == 15)),
                                waits=tw if first else [],
                                inc=cT if (tg == ntg - 1 and hc == 15) else None)
                            first = False
                    g_done = cT.n
                    firstu = True
                    for hc in range(16):
                        for tg in range(ntg):
                            s0 = tg * 512
                            rec.emit("T", lambda e, buf=buf, hc=hc, ic2=ic2, s0=s0: e.matmul(
                                pB[:, s0:s0 + 512],
                                wub[buf][:, hc * 256 + ic2 * 128:hc * 256 + ic2 * 128 + 128],
                                xnt[:, hc * T_CORE + t0 + s0: hc * T_CORE + t0 + s0 + 512],
                                start=(hc == 0), stop=(hc == 15)),
                                waits=[(cV, stt_mark.get(ic - 1, 0))] if firstu else [],
                                inc=cT if (tg == ntg - 1 and hc == 15) else None)
                            firstu = False
                    u_done = cT.n
                    rec.emit("A", lambda e, ic=ic: e.activation(
                        out=sgb[:, 0:TH], in_=pA[:, 0:TH], func=AF.Silu,
                        scale=r5[:, GSC + ic:GSC + ic + 1]),
                        waits=[(cT, g_done), (cV, stt_mark.get(ic - 1, 0))], inc=cA)
                    silu_mark[ic] = cA.n
                    rec.emit("V", lambda e, buf=buf, ic2=ic2, ic=ic: e.scalar_tensor_tensor(
                        hdn[buf][:, ic2 * TH:ic2 * TH + TH], pB[:, 0:TH],
                        r5[:, USC + ic:USC + ic + 1], sgb[:, 0:TH],
                        OP.mult, OP.mult),
                        waits=[(cT, u_done), (cA, silu_mark[ic])], inc=cV)
                    stt_mark[ic] = cV.n
                gu_mark[isb] = cT.n
                if isb % 2 == 1:
                    for hc in range(16):
                        rs = [(hc * ntg + tg) % 4 for tg in range(ntg)]
                        tw = [(cV, stt_mark[isb * 2 + 1]),
                              wd_ready[isb - 1], wd_ready[isb]]
                        tw += [(cV, pd_free.get(r, 0)) for r in rs]
                        first = True
                        for sb in (isb - 1, isb):
                            bu = sb % 2
                            for ic2 in range(2):
                                last = (sb == isb and ic2 == 1)
                                for tg in range(ntg):
                                    s0 = tg * 512
                                    r = rs[tg]
                                    rec.emit("T", lambda e, bu=bu, hc=hc, ic2=ic2, s0=s0, r=r, sb=sb, first_g=(sb == isb - 1 and ic2 == 0), last=last: e.matmul(
                                        pC[:, r * 512:r * 512 + 512],
                                        wdb[bu][:, ic2 * 2048 + hc * 128:ic2 * 2048 + hc * 128 + 128],
                                        hdn[bu][:, ic2 * TH + s0:ic2 * TH + s0 + 512],
                                        start=first_g, stop=last),
                                        waits=tw if first else [],
                                        inc=cT if (last and tg == ntg - 1) else None)
                                    first = False
                        dn_done = cT.n
                        for tg in range(ntg):
                            s0 = tg * 512
                            r = rs[tg]
                            rec.emit("V", lambda e, hc=hc, s0=s0, r=r: e.tensor_tensor(
                                yacc[:, hc * TH + s0:hc * TH + s0 + 512],
                                yacc[:, hc * TH + s0:hc * TH + s0 + 512],
                                pC[:, r * 512:r * 512 + 512], OP.add),
                                waits=[(cT, dn_done), (cP, yinit)]
                                if tg == 0 else [(cV, cV.n)],
                                inc=cV)
                            pd_free[r] = cV.n
                            yadd_mark[(isb, hc, tg)] = cV.n
                    mm_done[isb - 1] = cT.n
                mm_done[isb] = cT.n
            r6f = r6.bitcast(F32)
            r6b = r6
            xstore = {}
            silu_last = silu_mark[NISB * 2 - 1]
            stt_last = stt_mark[NISB * 2 - 1]
            stat_mm = {}
            for hc in range(16):
                b = hc % 2
                xst = r4[:, b * TH:b * TH + TH]
                xrow = yacc[:, hc * TH:hc * TH + TH]
                dma("P", xst, xsrc[hc * 128:(hc + 1) * 128, t0:t0 + TH], sg,
                    waits=[(st, xstore.get(hc - 2, e_st)),
                           (cV, cV.n if hc < 2 else 0)])
                ld = sg.n
                rec.emit("A", lambda e, hc=hc: e.activation(
                    out=r6f[:, 0:TH], in_=yacc[:, hc * TH:hc * TH + TH],
                    func=AF.Copy, scale=r5[:, DSCH + hc:DSCH + hc + 1]),
                    waits=[(cV, yadd_mark[(NISB - 1, hc, ntg - 1)]),
                           (cV, cV.n)], inc=cA)
                ys = cA.n
                rec.emit("V", lambda e, b=b, xrow=xrow: e.tensor_tensor(
                    xrow, r4[:, b * TH:b * TH + TH], r6f[:, 0:TH], OP.add),
                    waits=[(sg, ld), (cA, ys)], inc=cV)
                addm = cV.n
                dma("A", xdst[hc * 128:(hc + 1) * 128, t0:t0 + TH], xrow, st,
                    waits=[(cV, addm)])
                xstore[hc] = st.n
                if fuse_ln:
                    rec.emit("A", lambda e, xrow=xrow: e.activation(
                        out=r6b[:, 2048:2048 + TH], in_=xrow, func=AF.Copy),
                        waits=[(cV, addm), (cT, stat_mm.get(hc - 1, 0))], inc=cA)
                    rec.emit("A", lambda e: e.activation(
                        out=r6b[:, 3072:3072 + TH], in_=r6b[:, 2048:2048 + TH],
                        func=AF.Square), waits=[(cA, cA.n)], inc=cA)
                    sqh = cA.n
                    for tg in range(ntg):
                        s0 = tg * 512
                        rec.emit("T", lambda e, s0=s0, hc=hc: e.matmul(
                            pA[0:1, s0:s0 + 512], onesb[:, 0:1],
                            r6b[:, 2048 + s0:2048 + s0 + 512],
                            start=(hc == 0), stop=(hc == 15)),
                            waits=[(cA, sqh)] +
                                  ([(cA, silu_last), (cV, stt_last)]
                                   if (hc == 0 and tg == 0) else []))
                        rec.emit("T", lambda e, s0=s0, hc=hc: e.matmul(
                            pB[0:1, s0:s0 + 512], onesb[:, 0:1],
                            r6b[:, 3072 + s0:3072 + s0 + 512],
                            start=(hc == 0), stop=(hc == 15)),
                            inc=cT if tg == ntg - 1 else None)
                    stat_mm[hc] = cT.n
            if not fuse_ln:
                return st.n, 0
            stat_done = cT.n
            bc_mark = {}
            for tg in range(ntg):
                s0 = tg * 512
                w0 = [(cT, stat_done)]
                if tg > 0:
                    w0 += [(cT, bc_mark[tg - 1]), (cV, cV.n)]
                rec.emit("V", lambda e, s0=s0: e.tensor_scalar_mul(
                    r5[0:1, LNM:LNM + 512], pA[0:1, s0:s0 + 512], 1.0 / D),
                    waits=w0, inc=cV)
                rec.emit("V", lambda e, s0=s0: e.tensor_scalar_mul(
                    r5[0:1, LNV:LNV + 512], pB[0:1, s0:s0 + 512], 1.0 / D), inc=cV)
                rec.emit("V", lambda e, s0=s0: e.tensor_tensor(
                    pA[0:1, s0:s0 + 512], r5[0:1, LNM:LNM + 512],
                    r5[0:1, LNM:LNM + 512], OP.mult), waits=[(cV, cV.n)], inc=cV)
                rec.emit("V", lambda e, s0=s0: e.tensor_tensor(
                    r5[0:1, LNV:LNV + 512], r5[0:1, LNV:LNV + 512],
                    pA[0:1, s0:s0 + 512], OP.subtract), waits=[(cV, cV.n)], inc=cV)
                vdone = cV.n
                rec.emit("T", lambda e, s0=s0: e.matmul(
                    pA[:, s0:s0 + 512], onesrow[0:1, 0:128], r5[0:1, LNM:LNM + 512],
                    start=True, stop=True), waits=[(cV, vdone)], inc=cT)
                mb_mark = cT.n
                rec.emit("A", lambda e: e.activation(
                    out=r5[0:1, LNM:LNM + 512], in_=r5[0:1, LNV:LNV + 512],
                    func=AF.Sqrt, bias=r5[0:1, EPSs:EPSs + 1]),
                    waits=[(cV, vdone), (cT, mb_mark)], inc=cA)
                rec.emit("V", lambda e: e.reciprocal(
                    r5[0:1, LNM:LNM + 512], r5[0:1, LNM:LNM + 512]),
                    waits=[(cA, cA.n)], inc=cV)
                rec.emit("T", lambda e, s0=s0: e.matmul(
                    pB[:, s0:s0 + 512], onesrow[0:1, 0:128], r5[0:1, LNM:LNM + 512],
                    start=True, stop=True), waits=[(cV, cV.n)], inc=cT)
                bc_mark[tg] = cT.n
            for fc in range(16):
                xrow = yacc[:, fc * TH:fc * TH + TH]
                for tg in range(ntg):
                    s0 = tg * 512
                    rec.emit("V", lambda e, xrow=xrow, s0=s0: e.tensor_tensor(
                        xrow[:, s0:s0 + 512], xrow[:, s0:s0 + 512],
                        pA[:, s0:s0 + 512], OP.subtract),
                        waits=[(st, xstore[fc]), (cT, bc_mark[ntg - 1])]
                        if tg == 0 else [(cV, cV.n)], inc=cV)
                    rec.emit("V", lambda e, xrow=xrow, s0=s0, fc=fc: e.tensor_tensor(
                        xnt[:, fc * T_CORE + t0 + s0: fc * T_CORE + t0 + s0 + 512],
                        xrow[:, s0:s0 + 512], pB[:, s0:s0 + 512], OP.mult),
                        waits=[(cV, cV.n)], inc=cV)
            return st.n, cV.n

        def attention(xn_mark):
            ntg4 = T_CORE // 512
            wqb = [r3[:, WQB:WQB + 2048], r3[:, WQB + 2048:WQB + 4096]]
            kbst = r3f[:, QB_F:QB_F + T_CORE]
            swp = r3f[:, SWP_F:SWP_F + T_CORE]
            klocT = r6[:, 0:T_CORE]
            vlocT = r6[:, 2048:2048 + T_CORE]
            vstg = r3[:, VSTG:VSTG + T_CORE]
            cosq = r3f[:, TCOSQ:TCOSQ + T_CORE]
            sinq = r3f[:, TSINQ:TSINQ + T_CORE]
            cosk = r3f[:, TCOSK:TCOSK + T_CORE]
            sink = r3f[:, TSINK:TSINK + T_CORE]

            tw = [(cT, cT.n), (st, st.n), (cA, cA.n)]
            dma("P", cosq, cosq_in[:, :], sg, waits=tw)
            dma("P", sinq, sinq_in[:, :], sg)
            dma("P", cosk, cosk_in[:, :], sg)
            dma("P", sink, sink_in[:, :], sg)
            dma("P", wqb[0].rearrange("p (dc o) -> p dc o", o=128),
                wk.ap().rearrange("(dc p) o -> p dc o", p=128), swo,
                waits=[(cT, cT.n), (st, st.n), (cA, cA.n)])
            tabs = sg.n

            def rotary(dst_bf, src_f, cos_t, sin_t, vw):
                rec.emit("V", lambda e: e.tensor_copy(swp[0:64, :], src_f[64:128, :]),
                         waits=vw, inc=cV)
                rec.emit("V", lambda e: e.tensor_copy(swp[64:128, :], src_f[0:64, :]),
                         waits=[(cV, cV.n)], inc=cV)
                rec.emit("V", lambda e: e.tensor_tensor(src_f, src_f, cos_t, OP.mult),
                         waits=[(cV, cV.n)], inc=cV)
                rec.emit("V", lambda e: e.tensor_tensor(swp, swp, sin_t, OP.mult),
                         waits=[(cV, cV.n)], inc=cV)
                rec.emit("V", lambda e: e.tensor_tensor(dst_bf, src_f, swp, OP.add),
                         waits=[(cV, cV.n)], inc=cV)
                return cV.n

            for dc in range(16):
                for tg in range(ntg4):
                    s0 = tg * 512
                    rec.emit("T", lambda e, tg=tg, dc=dc, s0=s0: e.matmul(
                        pC[:, s0:s0 + 512],
                        wqb[0][:, dc * 128:dc * 128 + 128],
                        xnt[:, dc * T_CORE + s0:dc * T_CORE + s0 + 512],
                        start=(dc == 0), stop=(dc == 15)),
                        waits=[(sg, tabs), (swo, swo.n), (cV, xn_mark)]
                        if (tg == 0 and dc == 0) else [],
                        inc=cT if (dc == 15 and tg == ntg4 - 1) else None)
            kmm = cT.n
            rec.emit("V", lambda e: e.tensor_scalar_add(
                kbst, pC[:, 0:T_CORE], r5[:, BK:BK + 1]),
                waits=[(cT, kmm), (cV, cV.n)], inc=cV)
            kv_evac = cV.n
            krot = rotary(klocT, kbst, cosk, sink, [(cV, kv_evac)])
            dma("P", cc_ik[:, :], klocT, sg, waits=[(cV, krot)])
            kout = sg.n
            dma("P", wqb[1].rearrange("p (dc o) -> p dc o", o=128),
                wv.ap().rearrange("(dc p) o -> p dc o", p=128), swo)
            vw = swo.n
            for dc in range(16):
                for tg in range(ntg4):
                    s0 = tg * 512
                    rec.emit("T", lambda e, tg=tg, dc=dc, s0=s0: e.matmul(
                        pC[:, s0:s0 + 512],
                        wqb[1][:, dc * 128:dc * 128 + 128],
                        xnt[:, dc * T_CORE + s0:dc * T_CORE + s0 + 512],
                        start=(dc == 0), stop=(dc == 15)),
                        waits=[(swo, vw), (cV, kv_evac)]
                        if (tg == 0 and dc == 0) else [],
                        inc=cT if (dc == 15 and tg == ntg4 - 1) else None)
            vmm = cT.n
            rec.emit("V", lambda e: e.tensor_scalar_add(
                vlocT, pC[:, 0:T_CORE], r5[:, BV:BV + 1]),
                waits=[(cT, vmm), (cV, cV.n)], inc=cV)
            vev = cV.n
            ntt = T_CORE // 128
            tp_done = {}
            for tt in range(ntt):
                pb = pAb if tt % 2 == 0 else pBb
                rec.emit("T", lambda e, tt=tt, pb=pb: e.transpose(
                    pb[:, 0:128], vlocT[:, tt * 128:tt * 128 + 128], ident[:, :]),
                    waits=[(cV, vev) if tt < 2 else (cA, tp_done.get(tt - 2, 0))],
                    inc=cT)
                tpm = cT.n
                rec.emit("A", lambda e, tt=tt, pb=pb: e.activation(
                    out=vstg[:, tt * 128:tt * 128 + 128], in_=pb[:, 0:128],
                    func=AF.Copy), waits=[(cT, tpm)], inc=cA)
                tp_done[tt] = cA.n
            dma("P", cc_iv.ap().rearrange("(tt p) d -> p tt d", p=128),
                vstg.rearrange("p (tt d) -> p tt d", d=128), sg,
                waits=[(cA, cA.n)])
            vout = sg.n
            grp = [[0, 1, 2, 3], [4, 5, 6, 7]]
            rec.emit("P", lambda e: e.collective_compute(
                "AllGather", OP.bypass, replica_groups=grp,
                ins=[cc_ik.ap().opt()], outs=[cc_ok.ap().opt()]),
                waits=[(sg, kout)], inc=scc)
            rec.emit("P", lambda e: e.collective_compute(
                "AllGather", OP.bypass, replica_groups=grp,
                ins=[cc_iv.ap().opt()], outs=[cc_ov.ap().opt()]),
                waits=[(sg, vout)], inc=scc)
            cc_done = scc.n
            for r in range(4):
                dma("S", r3[:, KK + r * T_CORE:KK + (r + 1) * T_CORE],
                    cc_ok[r * 128:(r + 1) * 128, :], sw,
                    waits=[(scc, cc_done - 1), (cV, krot)] if r == 0 else [],
                    ng=(r == 0))
            kload = sw.n

            qmm_mark = {}
            prev_qev = cV.n
            nch = KEYS // 512
            kms = {}
            red = {}
            rec.emit("V", lambda e: e.memset(r5[0:1, KAP + 1:KAP + 2], 0.0),
                     waits=[(cV, cV.n)], inc=cV)
            kmax_chain = cV.n
            for h in range(NH):
                b = h % 2
                for ch in ([2 * (h - 8), 2 * (h - 8) + 1]
                           if (h >= 8 and 2 * (h - 8) + 1 < nch) else []):
                    kb = ch % 2
                    rec.emit("A", lambda e, ch=ch, kb=kb: e.activation(
                        out=pstg[:, kb * 512:kb * 512 + 512],
                        in_=r3[:, KK + ch * 512:KK + ch * 512 + 512],
                        func=AF.Square),
                        waits=[(sw, kload), (cT, kms.get(ch - 2, 0))], inc=cA)
                    sqch = cA.n
                    kp = pB if kb == 0 else pA
                    rec.emit("T", lambda e, kb=kb, kp=kp: e.matmul(
                        kp[0:1, 512:1024],
                        onesb[:, 0:1], pstg[:, kb * 512:kb * 512 + 512],
                        start=True, stop=True),
                        waits=[(cA, sqch), (cV, red.get(ch - 2, kmax_chain))],
                        inc=cT)
                    kms[ch] = cT.n
                    rec.emit("V", lambda e, kp=kp: e.reduce_max(
                        r5[0:1, TMPS + 2:TMPS + 3],
                        kp[0:1, 512:1024],
                        axis=mybir.AxisListType.X),
                        waits=[(cT, kms[ch]), (cV, cV.n)], inc=cV)
                    rec.emit("V", lambda e: e.tensor_tensor(
                        r5[0:1, KAP + 1:KAP + 2], r5[0:1, KAP + 1:KAP + 2],
                        r5[0:1, TMPS + 2:TMPS + 3], OP.max),
                        waits=[(cV, cV.n)], inc=cV)
                    red[ch] = cV.n
                dma("P", wqb[b].rearrange("p (dc o) -> p dc o", o=128),
                    wq.ap().rearrange(
                    "(dc p) o -> p dc o", p=128)[:, :, h * 128:h * 128 + 128], swo,
                    waits=[(cT, qmm_mark.get(h - 2, vmm if h < 2 else 0)),
                           (sg, vout if h == 0 else 0)])
                wql = swo.n
                for dc in range(16):
                    for tg in range(ntg4):
                        s0 = tg * 512
                        rec.emit("T", lambda e, b=b, dc=dc, s0=s0: e.matmul(
                            pC[:, s0:s0 + 512],
                            wqb[b][:, dc * 128:dc * 128 + 128],
                            xnt[:, dc * T_CORE + s0:dc * T_CORE + s0 + 512],
                            start=(dc == 0), stop=(dc == 15)),
                            waits=[(swo, wql), (cV, prev_qev)]
                            if (tg == 0 and dc == 0) else [],
                            inc=cT if (dc == 15 and tg == ntg4 - 1) else None)
                qmm_mark[h] = cT.n
                rec.emit("V", lambda e, h=h: e.tensor_scalar_add(
                    kbst, pC[:, 0:T_CORE], r5[:, BQ + h:BQ + h + 1]),
                    waits=[(cT, qmm_mark[h]), (cV, cV.n)], inc=cV)
                qev = cV.n
                prev_qev = qev
                rec.emit("A", lambda e: e.activation(
                    out=r4.bitcast(BF16)[:, 0:T_CORE], in_=kbst, func=AF.Square),
                    waits=[(cV, qev), (cT, cT.n)], inc=cA)
                sqd = cA.n
                vfree = cV.n
                qa = yacc_b[:, h * T_CORE:(h + 1) * T_CORE]
                qrot = rotary(qa, kbst, cosq, sinq, [(cV, qev), (cA, sqd)])
                rec.emit("V", lambda e, h=h: e.memset(
                    r5[0:1, R2M + h:R2M + h + 1], 0.0), waits=[(cV, vfree)], inc=cV)
                rmm = {}
                for j in range(ntg4):
                    p = pA if j % 2 == 0 else pB
                    rec.emit("T", lambda e, j=j, p=p: e.matmul(
                        p[0:1, 0:512], onesb[:, 0:1],
                        r4.bitcast(BF16)[:, j * 512:j * 512 + 512],
                        start=True, stop=True),
                        waits=[(cA, sqd), (cV, rmm.get(j - 2, cV.n))], inc=cT)
                    rj = cT.n
                    rec.emit("V", lambda e, p=p: e.reduce_max(
                        r5[0:1, TMPS:TMPS + 1], p[0:1, 0:512],
                        axis=mybir.AxisListType.X),
                        waits=[(cT, rj), (cV, cV.n)], inc=cV)
                    rec.emit("V", lambda e, h=h: e.tensor_tensor(
                        r5[0:1, R2M + h:R2M + h + 1], r5[0:1, R2M + h:R2M + h + 1],
                        r5[0:1, TMPS:TMPS + 1], OP.max), waits=[(cV, cV.n)], inc=cV)
                    rmm[j] = cV.n
            qphase_done = (cT.n, cA.n, cV.n)

            kw = [(scc, cc_done), (cV, qphase_done[2]), (cT, qphase_done[0])]
            dma("S", r3[:, VV:VV + NKT * 128].rearrange("p (kt d) -> p kt d", d=128),
                cc_ov.ap().rearrange("(kt p) d -> p kt d", p=128), sw, waits=kw)
            kvload = sw.n
            rec.emit("V", lambda e: e.memset(r5[0:1, KAP:KAP + 1], 0.0), inc=cV)
            rec.emit("A", lambda e: e.activation(
                out=r5[0:1, KAP:KAP + 1], in_=r5[0:1, KAP + 1:KAP + 2],
                func=AF.Sqrt), waits=[(cV, cV.n)], inc=cA)
            kapd = cA.n
            rec.emit("A", lambda e: e.activation(
                out=r5[0:1, TMPS:TMPS + 16], in_=r5[0:1, R2M:R2M + 16],
                func=AF.Sqrt, scale=1.0 / HD), waits=[(cA, kapd), (cV, cV.n)], inc=cA)
            rec.emit("V", lambda e: e.tensor_scalar_mul(
                r5[0:1, TMPS:TMPS + 16], r5[0:1, TMPS:TMPS + 16],
                r5[0:1, KAP:KAP + 1]), waits=[(cA, cA.n)], inc=cV)
            rec.emit("T", lambda e: e.matmul(
                pA[:, 0:16], onesrow[0:1, 0:128], r5[0:1, TMPS:TMPS + 16],
                start=True, stop=True), waits=[(cV, cV.n)], inc=cT)
            rec.emit("V", lambda e: e.tensor_scalar_mul(
                r5[:, MHN:MHN + 16], pA[:, 0:16], -1.0),
                waits=[(cT, cT.n)], inc=cV)
            mh_done = cV.n

            u = T_CORE // 2
            lacc8 = xnt[:, 0:T_CORE]
            lacc32 = xnt_f[:, u:u + T_CORE]
            linv = xnt_f[0:1, u + T_CORE:u + 2 * T_CORE]
            ctxs = xnt[:, 5 * T_CORE:6 * T_CORE]
            pring = [r3[:, PRING + i * QH:PRING + (i + 1) * QH] for i in range(8)]
            ctx_evac = {-1: (cA.n, cV.n)}
            prev_exp63 = {}
            prev_tail = {}
            for h in range(NH):
                qa = yacc_b[:, h * T_CORE:(h + 1) * T_CORE]
                exp_mark = {}
                pv_mark = {}
                l_mark = {}
                for qh in range(2):
                    ps = [pA, pB]
                    cslice = pC[:, qh * QH:qh * QH + QH]
                    for kt in range(NKT):
                        tw = []
                        if kt == 0 and qh == 0:
                            tw += [(sw, kvload), (cV, mh_done)]
                        if h > 0 and kt < 2 and qh == 0:
                            tw.append((cA, prev_exp63[(NKT - 2 + kt, qh)]))
                        if kt < 2 and qh == 1:
                            tw.append((cA, exp_mark[(NKT - 2 + kt, 0)]))
                        if kt >= 2:
                            tw.append((cA, exp_mark[(kt - 2, qh)]))
                        p = ps[kt % 2]
                        for j in range(QH // 512):
                            rec.emit("T", lambda e, kt=kt, j=j, p=p, qa=qa, qh=qh: e.matmul(
                                p[:, j * 512:j * 512 + 512],
                                r3[:, KK + kt * 128:KK + kt * 128 + 128],
                                qa[:, qh * QH + j * 512:qh * QH + j * 512 + 512],
                                start=True, stop=True),
                                waits=tw if j == 0 else [],
                                inc=cT if j == (QH // 512 - 1) else None)
                        sc = cT.n
                        if kt > 0:
                            slot = ((kt - 1) % 4) * 2 + qh
                            pvw = [(cA, exp_mark[(kt - 1, qh)])]
                            if kt == 1 and qh == 0:
                                pvw.append((cV, ctx_evac[h - 1][1]))
                            for j in range(QH // 512):
                                rec.emit("T", lambda e, kt=kt, j=j, slot=slot, cslice=cslice: e.matmul(
                                    cslice[:, j * 512:j * 512 + 512],
                                    r3[:, VV + (kt - 1) * 128:VV + (kt - 1) * 128 + 128],
                                    pring[slot][:, j * 512:j * 512 + 512],
                                    start=(kt - 1 == 0), stop=False),
                                    waits=pvw if j == 0 else [],
                                    inc=cT if j == (QH // 512 - 1) else None)
                            pv_mark[(kt - 1, qh)] = cT.n
                        slot = (kt % 4) * 2 + qh
                        ew = [(cT, sc), (cT, pv_mark.get((kt - 4, qh), 0)),
                              (cV, l_mark.get((kt - 4, qh), 0))]
                        if kt < 4 and h > 0:
                            ew += [(cT, prev_tail[(kt, qh)][0]),
                                   (cV, prev_tail[(kt, qh)][1])]
                        rec.emit("A", lambda e, h=h, p=p, slot=slot: e.activation(
                            out=pring[slot][:, 0:QH], in_=p[:, 0:QH], func=AF.Exp,
                            bias=r5[:, MHN + h:MHN + h + 1]),
                            waits=ew, inc=cA)
                        exp_mark[(kt, qh)] = cA.n
                        lw = [(cA, exp_mark[(kt, qh)])]
                        l8 = lacc8[:, qh * QH:qh * QH + QH]
                        if kt % 8 == 0:
                            rec.emit("V", lambda e, slot=slot, l8=l8: e.tensor_copy(
                                l8, pring[slot][:, 0:QH]),
                                waits=lw + [(cV, cV.n)], inc=cV)
                        else:
                            rec.emit("V", lambda e, slot=slot, l8=l8: e.tensor_tensor(
                                l8, l8, pring[slot][:, 0:QH], OP.add),
                                waits=lw + [(cV, cV.n)], inc=cV)
                        if kt % 8 == 7 or kt == NKT - 1:
                            l32 = lacc32[:, qh * QH:qh * QH + QH]
                            if kt < 8:
                                rec.emit("V", lambda e, l32=l32, l8=l8: e.tensor_copy(
                                    l32, l8), waits=[(cV, cV.n)], inc=cV)
                            else:
                                rec.emit("V", lambda e, l32=l32, l8=l8: e.tensor_tensor(
                                    l32, l32, l8, OP.add), waits=[(cV, cV.n)], inc=cV)
                        l_mark[(kt, qh)] = cV.n
                    slot = ((NKT - 1) % 4) * 2 + qh
                    for j in range(QH // 512):
                        rec.emit("T", lambda e, j=j, slot=slot, cslice=cslice: e.matmul(
                            cslice[:, j * 512:j * 512 + 512],
                            r3[:, VV + (NKT - 1) * 128:VV + (NKT - 1) * 128 + 128],
                            pring[slot][:, j * 512:j * 512 + 512],
                            start=(NKT == 1), stop=True),
                            waits=[(cA, exp_mark[(NKT - 1, qh)])] if j == 0 else [],
                            inc=cT if j == (QH // 512 - 1) else None)
                    pv_mark[(NKT - 1, qh)] = cT.n
                pv_all = cT.n
                lfold = cV.n
                for j in range(T_CORE // 512):
                    rec.emit("A", lambda e, j=j: e.activation(
                        out=ctxs[:, j * 512:j * 512 + 512],
                        in_=pC[:, j * 512:j * 512 + 512], func=AF.Copy),
                        waits=[(cT, pv_all)] if j == 0 else [], inc=cA)
                    cevj = cA.n
                    rec.emit("T", lambda e, j=j: e.matmul(
                        pC[0:1, j * 512:j * 512 + 512], onesf[:, 0:1],
                        lacc32[:, j * 512:j * 512 + 512], start=True, stop=True),
                        waits=[(cA, cevj)] + ([(cV, lfold)] if j == 0 else []),
                        inc=cT)
                    lrj = cT.n
                    rec.emit("V", lambda e, j=j: e.reciprocal(
                        linv[0:1, j * 512:j * 512 + 512],
                        pC[0:1, j * 512:j * 512 + 512]),
                        waits=[(cT, lrj), (cV, cV.n)], inc=cV)
                    rcpj = cV.n
                    rec.emit("T", lambda e, j=j: e.matmul(
                        pC[:, j * 512:j * 512 + 512], onesrow[0:1, 0:128],
                        linv[0:1, j * 512:j * 512 + 512], start=True, stop=True),
                        waits=[(cV, rcpj)], inc=cT)
                    lbj = cT.n
                    rec.emit("V", lambda e, j=j, qa=qa: e.tensor_tensor(
                        qa[:, j * 512:j * 512 + 512], ctxs[:, j * 512:j * 512 + 512],
                        pC[:, j * 512:j * 512 + 512], OP.mult),
                        waits=[(cT, lbj), (cV, cV.n)], inc=cV)
                ctx_evac[h] = (cA.n, cV.n)
                prev_exp63 = dict(exp_mark)
                prev_tail = {(kt, qh): (pv_mark[(NKT - 4 + kt, qh)],
                                        l_mark[(NKT - 4 + kt, qh)])
                             for kt in range(4) for qh in range(2)}

            wob = [r3[:, WOB:WOB + 2048], r3[:, WOB + 2048:WOB + 4096]]
            u = T_CORE // 2
            xtC = [r4[:, 0:T_CORE], xnt_f[:, u + 3 * T_CORE:u + 4 * T_CORE]]
            o_f = xnt_f[:, u + 2 * T_CORE:u + 3 * T_CORE]
            ommT = {}
            ommV = {}
            ostore = {}
            for mo in range(NH):
                b = mo % 2
                dma("P", wob[b].rearrange("p (dc o) -> p dc o", o=128),
                    wo.ap().rearrange(
                    "(dc p) o -> p dc o", p=128)[:, :, mo * 128:mo * 128 + 128], swo,
                    waits=[(cT, ommT.get(mo - 2, 0)),
                           (cV, ctx_evac[NH - 1][1] if mo == 0 else 0)])
                wol = swo.n
                dma("P", xtC[b], xs[mo * 128:(mo + 1) * 128, 0:T_CORE], sg,
                    waits=[(st, ostore.get(mo - 2, 0))])
                xld = sg.n
                tw = [(swo, wol), (cV, ommV.get(mo - 1, cV.n if mo == 0 else 0))]
                for hc in range(16):
                    for j in range(T_CORE // 512):
                        rec.emit("T", lambda e, b=b, j=j, hc=hc: e.matmul(
                            pC[:, j * 512:j * 512 + 512],
                            wob[b][:, hc * 128:hc * 128 + 128],
                            yacc_b[:, hc * T_CORE + j * 512:hc * T_CORE + j * 512 + 512],
                            start=(hc == 0), stop=(hc == 15)),
                            waits=tw if (j == 0 and hc == 0) else [],
                            inc=cT if (j == T_CORE // 512 - 1 and hc == 15) else None)
                omm = cT.n
                ommT[mo] = omm
                rec.emit("V", lambda e, mo=mo: e.tensor_scalar_add(
                    o_f, pC[:, 0:T_CORE], r5[:, BO + mo:BO + mo + 1]),
                    waits=[(cT, omm), (cV, cV.n)], inc=cV)
                ommV[mo] = cV.n
                rec.emit("V", lambda e, b=b: e.tensor_tensor(
                    xtC[b], xtC[b], o_f, OP.add),
                    waits=[(sg, xld), (cV, ommV[mo])], inc=cV)
                dma("A", xs[mo * 128:(mo + 1) * 128, :], xtC[b], st,
                    waits=[(cV, cV.n)])
                ostore[mo] = st.n
            fw = [(st, ostore[fc]) for fc in range(16)]
            xn3a = ln_half(xs, 0, fc_waits=fw, bf16_stage=True,
                           dep_waits=[(cA, ctx_evac[NH - 1][0])])
            xn3b = ln_half(xs, TH, psA=pC[:, 0:1024],
                           psB=pC[:, 1024:2048], fc_waits=fw, bf16_stage=True,
                           dep_waits=[(cT, ommT[NH - 1]), (cV, ommV[NH - 1])])
            return xn3a, xn3b

        init_phase()
        prologue_casts()
        m = ln_half(x_in, 0, dep_waits=[(cV, cV.n), (cP, cP.n)])
        _, xn2a = ffn_half(1, 0, x_in, xs, m, direct=True, fuse_ln=(STAGE >= 2))
        m = ln_half(x_in, TH)
        _, xn2b = ffn_half(1, 1, x_in, xs, m, fuse_ln=(STAGE >= 2),
                           fresh=False)
        if STAGE >= 2:
            xn3a, xn3b = attention(max(xn2a, xn2b))
        else:
            xn3a = xn3b = 0
        if STAGE >= 3:
            ffn_half(3, 0, xs, out_fm, xn3a)
            ffn_half(3, 1, xs, out_fm, xn3b, fresh=False)
        else:
            for fc in range(FC):
                b = fc % 2
                dma("P", r4[:, b * 1024:b * 1024 + 1024],
                    xs[fc * 128:(fc + 1) * 128, 0:1024], sg,
                    waits=[(st, st.n)])
                dma("P", out_fm[fc * 128:(fc + 1) * 128, 0:1024],
                    r4[:, b * 1024:b * 1024 + 1024], sg)
                if T_CORE > 1024:
                    dma("P", r4[:, b * 1024:b * 1024 + 1024],
                        xs[fc * 128:(fc + 1) * 128, 1024:2048], sg)
                    dma("P", out_fm[fc * 128:(fc + 1) * 128, 1024:2048],
                        r4[:, b * 1024:b * 1024 + 1024], sg)

        rec.emit("A", None, waits=[(st, st.n)])
        rec.emit("P", None, waits=[(sg, sg.n), (st, st.n), (spc, spc.n),
                                   (swo, swo.n)])
        rec.emit("S", None, waits=[(sw, sw.n)])

        @block.tensor
        def _(t):
            for f in rec.ops["T"]:
                f(t)

        @block.scalar
        def _(a):
            for f in rec.ops["A"]:
                f(a)

        @block.vector
        def _(v):
            for f in rec.ops["V"]:
                f(v)

        @block.gpsimd
        def _(g):
            for f in rec.ops["P"]:
                f(g)

        @block.sync
        def _(s):
            for f in rec.ops["S"]:
                f(s)

    return nc


def make_core_inputs(inputs, core, T_CORE):
    b = core // 4
    q0 = (core % 4) * T_CORE
    bf = ml_dtypes.bfloat16
    x = np.asarray(inputs["L0_attn_x"], np.float32)
    x_fm = np.ascontiguousarray(x[b, q0:q0 + T_CORE, :].T)
    pos = (q0 + np.arange(T_CORE, dtype=np.float64))[None, :]
    dim = np.arange(64, dtype=np.float64)[:, None]
    ang = pos * (10000.0 ** (-2.0 * dim / 128.0))
    cos = np.cos(ang)
    sin = np.sin(ang)
    s = 1.0 / np.sqrt(128.0)
    cosq = (np.concatenate([cos, cos], 0) * s).astype(np.float32)
    sinq = (np.concatenate([-sin, sin], 0) * s).astype(np.float32)
    cosk = np.concatenate([cos, cos], 0).astype(np.float32)
    sink = np.concatenate([-sin, sin], 0).astype(np.float32)
    d = {
        "x_fm": x_fm,
        "gate_w": np.ascontiguousarray(inputs["L0_ffn_gate_w"], np.int32),
        "up_w": np.ascontiguousarray(inputs["L0_ffn_up_w"], np.int32),
        "down_w": np.ascontiguousarray(inputs["L0_ffn_down_w"], np.int32),
        "wq": np.ascontiguousarray(inputs["L0_attn_Wq"], np.float32),
        "wk": np.ascontiguousarray(inputs["L0_attn_Wk"], np.float32),
        "wv": np.ascontiguousarray(inputs["L0_attn_Wv"], np.float32),
        "wo": np.ascontiguousarray(inputs["L0_attn_Wo"], np.float32),
        "gsc": np.ascontiguousarray(
            np.asarray(inputs["L0_ffn_gate_scale"], np.float32).reshape(64, 128).T),
        "usc": np.ascontiguousarray(
            np.asarray(inputs["L0_ffn_up_scale"], np.float32).reshape(64, 128).T),
        "dsc": np.ascontiguousarray(
            np.asarray(inputs["L0_ffn_down_scale"], np.float32).reshape(16, 128).T),
        "bq": np.ascontiguousarray(
            np.asarray(inputs["L0_attn_bq"], np.float32).reshape(16, 128).T),
        "bk": np.asarray(inputs["L0_attn_bk"], np.float32).reshape(128, 1).copy(),
        "bv": np.asarray(inputs["L0_attn_bv"], np.float32).reshape(128, 1).copy(),
        "bo": np.ascontiguousarray(
            np.asarray(inputs["L0_attn_bo"], np.float32).reshape(16, 128).T),
        "cosq": cosq, "sinq": sinq, "cosk": cosk, "sink": sink,
        "ident": np.eye(128, dtype=bf),
    }
    return d


def kernel(**inputs):
    T_CORE = inputs["L0_attn_x"].shape[1] // 4
    nc = build(T_CORE)
    in_maps = [make_core_inputs(inputs, c, T_CORE) for c in range(8)]
    res = run_bass_kernel_spmd(nc, in_maps, core_ids=list(range(8)))
    B, T = inputs["L0_attn_x"].shape[0], inputs["L0_attn_x"].shape[1]
    out = np.empty((B, T, D), np.float32)
    for c in range(8):
        b = c // 4
        q0 = (c % 4) * T_CORE
        out[b, q0:q0 + T_CORE, :] = np.asarray(res.results[c]["out_fm"]).T
    return out


# version 14
# speedup vs baseline: 1.0136x; 1.0136x over previous
import numpy as np
import ml_dtypes
import concourse.bass as bass
import concourse.mybir as mybir
from concourse.bass_utils import run_bass_kernel_spmd

F32 = mybir.dt.float32
BF16 = mybir.dt.bfloat16
I32 = mybir.dt.int32
AF = mybir.ActivationFunctionType
OP = mybir.AluOpType

D = 2048
I_DIM = 8192
NH = 16
HD = 128
FC = D // 128
NISB = 32
ISB_I = I_DIM // NISB
EPS = 1e-5
STAGE = 3


class Sem:
    __slots__ = ("name", "n", "h", "base")

    def __init__(self, name):
        self.name = name
        self.n = 0
        self.base = 0
        self.h = None


class Rec:
    def __init__(self):
        self.ops = {k: [] for k in "TAVPS"}

    def emit(self, ek, fn, waits=(), inc=None, amt=1):
        ws = tuple((s, v) for (s, v) in waits)
        if inc is not None:
            inc.n += amt
        def run(eng, ws=ws, fn=fn, inc=inc, amt=amt):
            for s, v in ws:
                if v > 0:
                    eng.wait_ge(s.h, v)
            ins = fn(eng) if fn is not None else None
            if inc is not None:
                ins.then_inc(inc.h, amt)
        self.ops[ek].append(run)


def build(T_CORE):
    TH = T_CORE // 2
    NTG = TH // 512
    KEYS = 4 * T_CORE
    NKT = KEYS // 128
    QH = T_CORE // 2
    assert NTG >= 1 and QH <= 1024

    nc = bass.Bass("TRN2", target_bir_lowering=False, debug=False)

    x_in = nc.dram_tensor("x_fm", [D, T_CORE], F32, kind="ExternalInput")
    gate_w = nc.dram_tensor("gate_w", [I_DIM, D], I32, kind="ExternalInput")
    up_w = nc.dram_tensor("up_w", [I_DIM, D], I32, kind="ExternalInput")
    down_w = nc.dram_tensor("down_w", [D, I_DIM], I32, kind="ExternalInput")
    wq = nc.dram_tensor("wq", [D, D], F32, kind="ExternalInput")
    wk = nc.dram_tensor("wk", [D, HD], F32, kind="ExternalInput")
    wv = nc.dram_tensor("wv", [D, HD], F32, kind="ExternalInput")
    wo = nc.dram_tensor("wo", [D, D], F32, kind="ExternalInput")
    gsc_in = nc.dram_tensor("gsc", [128, 64], F32, kind="ExternalInput")
    usc_in = nc.dram_tensor("usc", [128, 64], F32, kind="ExternalInput")
    dsc_in = nc.dram_tensor("dsc", [128, 16], F32, kind="ExternalInput")
    bq_in = nc.dram_tensor("bq", [128, 16], F32, kind="ExternalInput")
    bk_in = nc.dram_tensor("bk", [128, 1], F32, kind="ExternalInput")
    bv_in = nc.dram_tensor("bv", [128, 1], F32, kind="ExternalInput")
    bo_in = nc.dram_tensor("bo", [128, 16], F32, kind="ExternalInput")
    cosq_in = nc.dram_tensor("cosq", [128, T_CORE], F32, kind="ExternalInput")
    sinq_in = nc.dram_tensor("sinq", [128, T_CORE], F32, kind="ExternalInput")
    cosk_in = nc.dram_tensor("cosk", [128, T_CORE], F32, kind="ExternalInput")
    sink_in = nc.dram_tensor("sink", [128, T_CORE], F32, kind="ExternalInput")
    ident_in = nc.dram_tensor("ident", [128, 128], BF16, kind="ExternalInput")

    out_fm = nc.dram_tensor("out_fm", [D, T_CORE], F32, kind="ExternalOutput")

    xs = nc.dram_tensor("xs", [D, T_CORE], F32)
    scr_g = nc.dram_tensor("scr_g", [I_DIM, D], BF16)
    scr_u = nc.dram_tensor("scr_u", [I_DIM, D], BF16)
    scr_d = nc.dram_tensor("scr_d", [NISB * D, ISB_I], BF16)
    scr2_g = nc.dram_tensor("scr2_g", [NISB * 128, 16 * ISB_I], BF16)
    scr2_u = nc.dram_tensor("scr2_u", [NISB * 128, 16 * ISB_I], BF16)
    scr2_d = nc.dram_tensor("scr2_d", [NISB * 128, 2 * D], BF16)
    cc_ik = nc.dram_tensor("cc_ik", [128, T_CORE], BF16)
    cc_ok = nc.dram_tensor("cc_ok", [512, T_CORE], BF16)
    cc_iv = nc.dram_tensor("cc_iv", [T_CORE, 128], BF16)
    cc_ov = nc.dram_tensor("cc_ov", [4 * T_CORE, 128], BF16)

    rec = Rec()
    sw, sg, st, scc = Sem("sw"), Sem("sg"), Sem("st"), Sem("scc")
    spc, swo, std, swd = Sem("spc"), Sem("swo"), Sem("std"), Sem("swd")
    cT, cA, cV, cP = Sem("cT"), Sem("cA"), Sem("cV"), Sem("cP")
    sems = [sw, sg, st, scc, spc, swo, std, swd, cT, cA, cV, cP]

    from contextlib import ExitStack
    with ExitStack() as ctx:
        xnt = ctx.enter_context(nc.sbuf_tensor("xnt", [128, FC * T_CORE], BF16))
        yacc = ctx.enter_context(nc.sbuf_tensor("yacc", [128, 16 * TH], F32))
        r3 = ctx.enter_context(nc.sbuf_tensor("r3", [128, 28672], BF16))
        r4 = ctx.enter_context(nc.sbuf_tensor("r4", [128, 2048], F32))
        r5 = ctx.enter_context(nc.sbuf_tensor("r5", [128, 1280], F32))
        r6 = ctx.enter_context(nc.sbuf_tensor("r6", [128, 4096], BF16))
        onesb = ctx.enter_context(nc.sbuf_tensor("onesb", [128, 2], BF16))
        onesf = ctx.enter_context(nc.sbuf_tensor("onesf", [128, 1], F32))
        onesrow = ctx.enter_context(nc.sbuf_tensor("onesrow", [1, 128], F32))
        ident = ctx.enter_context(nc.sbuf_tensor("ident_sb", [128, 128], BF16))
        pstg = ctx.enter_context(nc.sbuf_tensor("pstg", [128, 1024], BF16))
        pA = ctx.enter_context(nc.psum_tensor("pA", [128, 1024], F32))
        pB = ctx.enter_context(nc.psum_tensor("pB", [128, 1024], F32))
        pC = ctx.enter_context(nc.psum_tensor("pC", [128, 2048], F32))
        for s in sems:
            s.h = ctx.enter_context(nc.semaphore(s.name))
        block = ctx.enter_context(nc.Block())

        xnt_f = xnt.bitcast(F32)
        yacc_b = yacc.bitcast(BF16)
        r3f = r3.bitcast(F32)
        pAb = pA.bitcast(BF16)
        pBb = pB.bitcast(BF16)

        GSC, USC, DSC, DSCH = 0, 64, 128, 144
        BQ, BO, BK, BV = 160, 176, 192, 193
        EPSs, ONESF_UNUSED, MHN, R2M, KAP, TMPS = 194, 195, 196, 212, 228, 229
        LNM, LNV = 256, 768

        WG, WU, WD, HDN = 0, 8192, 16384, 24576
        KK, VV, WOB, PRING, CTXS = 8192, 0, 16384, 20480, 24576
        TCOSQ, TSINQ, TCOSK, TSINK = 0, 2048, 4096, 6144
        QB_F, SWP_F = 10240, 12288
        WQB = 16384
        VSTG = 20480

        def dma(ek, out, in_, sem, waits=(), ng=True, **kw):
            if ng:
                sem.base = sem.n
            waits = list(waits) + [(sem, sem.base)]
            rec.emit(ek, lambda e, o=out, i=in_, kw=kw: e.dma_start(out=o, in_=i, **kw),
                     waits=waits, inc=sem, amt=16)

        def init_phase():
            dma("P", r5[:, GSC:GSC + 64], gsc_in[:, :], sg)
            dma("P", r5[:, USC:USC + 64], usc_in[:, :], sg, ng=False)
            dma("P", r5[:, DSC:DSC + 16], dsc_in[:, :], sg, ng=False)
            dma("P", r5[:, BQ:BQ + 16], bq_in[:, :], sg, ng=False)
            dma("P", r5[:, BK:BK + 1], bk_in[:, :], sg, ng=False)
            dma("P", r5[:, BV:BV + 1], bv_in[:, :], sg, ng=False)
            dma("P", r5[:, BO:BO + 16], bo_in[:, :], sg, ng=False)
            dma("P", ident[:, :], ident_in[:, :], sg, ng=False)
            rec.emit("P", lambda e: e.memset(onesb[:, :], 1.0), inc=cP)
            rec.emit("P", lambda e: e.memset(onesf[:, :], 1.0), inc=cP)
            rec.emit("P", lambda e: e.memset(onesrow[:, :], 1.0), inc=cP)
            rec.emit("P", lambda e: e.memset(r5[:, EPSs:EPSs + 1], EPS), inc=cP)
            rec.emit("V", lambda e: e.tensor_scalar_mul(
                r5[:, DSCH:DSCH + 16], r5[:, DSC:DSC + 16], 0.5),
                waits=[(sg, sg.n)], inc=cV)

        cast_mark = {}

        def prologue_casts():
            for isb in range(NISB):
                r0 = isb * ISB_I
                dma("P", scr_g[r0:r0 + ISB_I, :], gate_w[r0:r0 + ISB_I, :], spc)
                dma("P", scr_u[r0:r0 + ISB_I, :], up_w[r0:r0 + ISB_I, :], spc, ng=False)
                dma("P", scr_d[isb * D:(isb + 1) * D, :], down_w[:, r0:r0 + ISB_I],
                    spc, ng=False)
                cast_mark[isb] = spc.n

        def ln_half(src, t0, dep_waits=(), stage=None, psA=None, psB=None,
                    fc_waits=None, bf16_stage=False):
            ntg = TH // 512
            e_cV, e_cT, e_st = cV.n, cT.n, st.n
            xbf = [r6[:, 0:TH], r6[:, 1024:1024 + TH]]
            sqb = [r6[:, 2048:2048 + TH], r6[:, 3072:3072 + TH]]
            xst = stage if stage is not None else [r4[:, 0:TH], r4[:, 1024:1024 + TH]]
            if psA is None:
                psA, psB = pA, pB
            mm_done = {}
            for fc in range(FC):
                b = fc % 2
                w = list(dep_waits) if fc < 2 else []
                if fc_waits is not None:
                    w.append(fc_waits[fc])
                if fc < 2:
                    w.append((cV, e_cV))
                    if fc_waits is None:
                        w.append((st, e_st))
                if fc >= 2:
                    w.append((cT, mm_done[fc - 2]))
                if bf16_stage:
                    dma("P", xbf[b], src[fc * 128:(fc + 1) * 128, t0:t0 + TH], sg,
                        waits=w + [(cT, mm_done.get(fc - 2, e_cT))])
                    rec.emit("A", lambda e, b=b: e.activation(
                        out=sqb[b], in_=xbf[b], func=AF.Square),
                        waits=[(sg, sg.n), (cA, cA.n),
                               (cV, e_cV if fc < 2 else 0)], inc=cA)
                else:
                    dma("P", xst[b], src[fc * 128:(fc + 1) * 128, t0:t0 + TH], sg,
                        waits=w)
                    rec.emit("A", lambda e, b=b: e.activation(
                        out=xbf[b], in_=xst[b], func=AF.Copy),
                        waits=[(sg, sg.n), (cT, mm_done.get(fc - 2, e_cT)),
                               (cV, e_cV if fc < 2 else 0)], inc=cA)
                    rec.emit("A", lambda e, b=b: e.activation(
                        out=sqb[b], in_=xbf[b], func=AF.Square),
                        waits=[(cA, cA.n)], inc=cA)
                aw = cA.n
                for tg in range(ntg):
                    rec.emit("T", lambda e, b=b, tg=tg, fc=fc: e.matmul(
                        psA[0:1, tg * 512:tg * 512 + 512], onesb[:, 0:1],
                        xbf[b][:, tg * 512:tg * 512 + 512],
                        start=(fc == 0), stop=(fc == FC - 1)),
                        waits=[(cA, aw)] if tg == 0 else [])
                    rec.emit("T", lambda e, b=b, tg=tg, fc=fc: e.matmul(
                        psB[0:1, tg * 512:tg * 512 + 512], onesb[:, 0:1],
                        sqb[b][:, tg * 512:tg * 512 + 512],
                        start=(fc == 0), stop=(fc == FC - 1)),
                        inc=cT if (tg == ntg - 1) else None)
                mm_done[fc] = cT.n
            stat_done = cT.n
            bc_mark = {}
            for tg in range(ntg):
                s0 = tg * 512
                w0 = [(cT, stat_done)]
                if tg > 0:
                    w0 += [(cT, bc_mark[tg - 1]), (cV, cV.n)]
                rec.emit("V", lambda e, s0=s0: e.tensor_scalar_mul(
                    r5[0:1, LNM:LNM + 512], psA[0:1, s0:s0 + 512], 1.0 / D),
                    waits=w0, inc=cV)
                rec.emit("V", lambda e, s0=s0: e.tensor_scalar_mul(
                    r5[0:1, LNV:LNV + 512], psB[0:1, s0:s0 + 512], 1.0 / D), inc=cV)
                rec.emit("V", lambda e, s0=s0: e.tensor_tensor(
                    psA[0:1, s0:s0 + 512], r5[0:1, LNM:LNM + 512],
                    r5[0:1, LNM:LNM + 512], OP.mult), waits=[(cV, cV.n)], inc=cV)
                rec.emit("V", lambda e, s0=s0: e.tensor_tensor(
                    r5[0:1, LNV:LNV + 512], r5[0:1, LNV:LNV + 512],
                    psA[0:1, s0:s0 + 512], OP.subtract), waits=[(cV, cV.n)], inc=cV)
                vdone = cV.n
                rec.emit("T", lambda e, s0=s0: e.matmul(
                    psA[:, s0:s0 + 512], onesrow[0:1, 0:128], r5[0:1, LNM:LNM + 512],
                    start=True, stop=True), waits=[(cV, vdone)], inc=cT)
                mb_mark = cT.n
                rec.emit("A", lambda e: e.activation(
                    out=r5[0:1, LNM:LNM + 512], in_=r5[0:1, LNV:LNV + 512],
                    func=AF.Sqrt, bias=r5[0:1, EPSs:EPSs + 1]),
                    waits=[(cV, vdone), (cT, mb_mark)], inc=cA)
                rec.emit("V", lambda e: e.reciprocal(
                    r5[0:1, LNM:LNM + 512], r5[0:1, LNM:LNM + 512]),
                    waits=[(cA, cA.n)], inc=cV)
                rec.emit("T", lambda e, s0=s0: e.matmul(
                    psB[:, s0:s0 + 512], onesrow[0:1, 0:128], r5[0:1, LNM:LNM + 512],
                    start=True, stop=True), waits=[(cV, cV.n)], inc=cT)
                bc_mark[tg] = cT.n
            e2_cT = cT.n
            st_dep = {}
            pf = pstg.bitcast(F32)
            for fc in range(FC):
                b = fc % 2
                w = [(cV, st_dep.get(fc - 2, 0))]
                if fc < 2:
                    w.append((cT, bc_mark[ntg - 1]))
                if bf16_stage:
                    dma("P", xbf[b], src[fc * 128:(fc + 1) * 128, t0:t0 + TH], sg,
                        waits=w)
                    ld = sg.n
                    for tg in range(ntg):
                        s0 = tg * 512
                        rec.emit("V", lambda e, b=b, s0=s0: e.tensor_tensor(
                            pf[:, 0:512], xbf[b][:, s0:s0 + 512],
                            psA[:, s0:s0 + 512], OP.subtract),
                            waits=[(sg, ld), (cT, e2_cT), (cV, cV.n)], inc=cV)
                        rec.emit("V", lambda e, s0=s0, fc=fc: e.tensor_tensor(
                            xnt[:, fc * T_CORE + t0 + s0: fc * T_CORE + t0 + s0 + 512],
                            pf[:, 0:512], psB[:, s0:s0 + 512], OP.mult),
                            waits=[(cV, cV.n)], inc=cV)
                else:
                    dma("P", xst[b], src[fc * 128:(fc + 1) * 128, t0:t0 + TH], sg,
                        waits=w)
                    ld = sg.n
                    for tg in range(ntg):
                        s0 = tg * 512
                        rec.emit("V", lambda e, b=b, s0=s0: e.tensor_tensor(
                            xst[b][:, s0:s0 + 512], xst[b][:, s0:s0 + 512],
                            psA[:, s0:s0 + 512], OP.subtract),
                            waits=[(sg, ld), (cT, e2_cT)], inc=cV)
                        rec.emit("V", lambda e, b=b, s0=s0, fc=fc: e.tensor_tensor(
                            xnt[:, fc * T_CORE + t0 + s0: fc * T_CORE + t0 + s0 + 512],
                            xst[b][:, s0:s0 + 512], psB[:, s0:s0 + 512], OP.mult),
                            waits=[(cV, cV.n)], inc=cV)
                st_dep[fc] = cV.n
            return cV.n

        def ffn_half(app, half, xsrc, xdst, xn_mark, direct=False, fuse_ln=False,
                     fresh=True):
            t0 = half * TH
            ntg = TH // 512
            e_cT, e_cV, e_cA, e_st = cT.n, cV.n, cA.n, st.n
            wgb = [r3[:, WG:WG + 4096], r3[:, WG + 4096:WG + 8192]]
            wub = [r3[:, WU:WU + 4096], r3[:, WU + 4096:WU + 8192]]
            wdb = [r3[:, WD:WD + 4096], r3[:, WD + 4096:WD + 8192]]
            hdn = [r3[:, HDN:HDN + 2048], r3[:, HDN + 2048:HDN + 4096]]
            sgb = r6[:, 0:TH]
            rec.emit("P", lambda e: e.memset(yacc[:, :], 0.0),
                     waits=[(cA, cA.n), (cV, cV.n), (st, st.n)], inc=cP)
            yinit = cP.n
            mm_done = {}
            gu_mark = {}
            w_ready = {}
            wd_ready = {}
            yadd_mark = {}
            pd_free = {}
            stt_mark = {}
            silu_mark = {}
            scr2w = {}
            for isb in range(NISB):
                buf = isb % 2
                war = [(cT, gu_mark.get(isb - 2, e_cT))]
                wdw = [(cT, mm_done.get(isb - 2, e_cT))]
                if isb < 2 and fresh:
                    war += [(cV, e_cV), (cA, e_cA)]
                    wdw += [(cV, e_cV), (cA, e_cA)]
                if direct:
                    war += [(spc, cast_mark[isb]), (st, scr2w.get(isb - 2, e_st))]
                    wdw += [(spc, cast_mark[isb]), (st, scr2w.get(isb - 2, e_st))]
                    for hc in range(16):
                        dma("S", wgb[buf][:, hc * 256:hc * 256 + 256],
                            scr_g[isb * 256:isb * 256 + 256, hc * 128:hc * 128 + 128],
                            sw, waits=war if hc == 0 else [],
                            ng=(hc == 0), transpose=True)
                        dma("S", wub[buf][:, hc * 256:hc * 256 + 256],
                            scr_u[isb * 256:isb * 256 + 256, hc * 128:hc * 128 + 128],
                            sw, ng=False, transpose=True)
                    ldm = sw.n
                    for wb in (wgb, wub):
                        rec.emit("A", lambda e, wb=wb, buf=buf: e.activation(
                            out=wb[buf], in_=wb[buf], func=AF.Copy, bias=-128.0),
                            waits=[(sw, ldm)], inc=cA)
                    negm = cA.n
                    for ic2 in range(2):
                        dma("S", wdb[buf][:, ic2 * 2048:ic2 * 2048 + 2048],
                            scr_d[isb * D:(isb + 1) * D, ic2 * 128:ic2 * 128 + 128],
                            swd, waits=wdw if ic2 == 0 else [],
                            ng=(ic2 == 0), transpose=True)
                    ldmd = swd.n
                    rec.emit("A", lambda e, buf=buf: e.activation(
                        out=wdb[buf], in_=wdb[buf], func=AF.Copy, bias=-128.0),
                        waits=[(swd, ldmd)], inc=cA)
                    negmd = cA.n
                    dma("A", scr2_g[isb * 128:(isb + 1) * 128, :], wgb[buf], st,
                        waits=[(cA, negm)])
                    dma("A", scr2_u[isb * 128:(isb + 1) * 128, :], wub[buf], st,
                        ng=False)
                    dma("A", scr2_d[isb * 128:(isb + 1) * 128, :], wdb[buf], st,
                        ng=False, waits=[(cA, negmd)])
                    scr2w[isb] = st.n
                    w_ready[isb] = (cA, negm)
                    wd_ready[isb] = (cA, negmd)
                else:
                    dma("S", wgb[buf], scr2_g[isb * 128:(isb + 1) * 128, :], sw,
                        waits=war)
                    dma("S", wub[buf], scr2_u[isb * 128:(isb + 1) * 128, :], sw,
                        ng=False)
                    w_ready[isb] = (sw, sw.n)
                    dma("S", wdb[buf], scr2_d[isb * 128:(isb + 1) * 128, :], swd,
                        waits=wdw)
                    wd_ready[isb] = (swd, swd.n)
                for ic2 in range(2):
                    ic = isb * 2 + ic2
                    tw = [w_ready[isb], (cA, silu_mark.get(ic - 1, 0)),
                          (cV, xn_mark)]
                    first = True
                    for hc in range(16):
                        for tg in range(ntg):
                            s0 = tg * 512
                            rec.emit("T", lambda e, buf=buf, hc=hc, ic2=ic2, s0=s0: e.matmul(
                                pA[:, s0:s0 + 512],
                                wgb[buf][:, hc * 256 + ic2 * 128:hc * 256 + ic2 * 128 + 128],
                                xnt[:, hc * T_CORE + t0 + s0: hc * T_CORE + t0 + s0 + 512],
                                start=(hc == 0), stop=(hc == 15)),
                                waits=tw if first else [],
                                inc=cT if (tg == ntg - 1 and hc == 15) else None)
                            first = False
                    g_done = cT.n
                    firstu = True
                    for hc in range(16):
                        for tg in range(ntg):
                            s0 = tg * 512
                            rec.emit("T", lambda e, buf=buf, hc=hc, ic2=ic2, s0=s0: e.matmul(
                                pB[:, s0:s0 + 512],
                                wub[buf][:, hc * 256 + ic2 * 128:hc * 256 + ic2 * 128 + 128],
                                xnt[:, hc * T_CORE + t0 + s0: hc * T_CORE + t0 + s0 + 512],
                                start=(hc == 0), stop=(hc == 15)),
                                waits=[(cV, stt_mark.get(ic - 1, 0))] if firstu else [],
                                inc=cT if (tg == ntg - 1 and hc == 15) else None)
                            firstu = False
                    u_done = cT.n
                    rec.emit("A", lambda e, ic=ic: e.activation(
                        out=sgb[:, 0:TH], in_=pA[:, 0:TH], func=AF.Silu,
                        scale=r5[:, GSC + ic:GSC + ic + 1]),
                        waits=[(cT, g_done), (cV, stt_mark.get(ic - 1, 0))], inc=cA)
                    silu_mark[ic] = cA.n
                    rec.emit("V", lambda e, buf=buf, ic2=ic2, ic=ic: e.scalar_tensor_tensor(
                        hdn[buf][:, ic2 * TH:ic2 * TH + TH], pB[:, 0:TH],
                        r5[:, USC + ic:USC + ic + 1], sgb[:, 0:TH],
                        OP.mult, OP.mult),
                        waits=[(cT, u_done), (cA, silu_mark[ic])], inc=cV)
                    stt_mark[ic] = cV.n
                gu_mark[isb] = cT.n
                if isb % 2 == 1:
                    for hc in range(16):
                        rs = [(hc * ntg + tg) % 4 for tg in range(ntg)]
                        tw = [(cV, stt_mark[isb * 2 + 1]),
                              wd_ready[isb - 1], wd_ready[isb]]
                        tw += [(cV, pd_free.get(r, 0)) for r in rs]
                        first = True
                        for sb in (isb - 1, isb):
                            bu = sb % 2
                            for ic2 in range(2):
                                last = (sb == isb and ic2 == 1)
                                for tg in range(ntg):
                                    s0 = tg * 512
                                    r = rs[tg]
                                    rec.emit("T", lambda e, bu=bu, hc=hc, ic2=ic2, s0=s0, r=r, sb=sb, first_g=(sb == isb - 1 and ic2 == 0), last=last: e.matmul(
                                        pC[:, r * 512:r * 512 + 512],
                                        wdb[bu][:, ic2 * 2048 + hc * 128:ic2 * 2048 + hc * 128 + 128],
                                        hdn[bu][:, ic2 * TH + s0:ic2 * TH + s0 + 512],
                                        start=first_g, stop=last),
                                        waits=tw if first else [],
                                        inc=cT if (last and tg == ntg - 1) else None)
                                    first = False
                        dn_done = cT.n
                        for tg in range(ntg):
                            s0 = tg * 512
                            r = rs[tg]
                            rec.emit("V", lambda e, hc=hc, s0=s0, r=r: e.tensor_tensor(
                                yacc[:, hc * TH + s0:hc * TH + s0 + 512],
                                yacc[:, hc * TH + s0:hc * TH + s0 + 512],
                                pC[:, r * 512:r * 512 + 512], OP.add),
                                waits=[(cT, dn_done), (cP, yinit)]
                                if tg == 0 else [(cV, cV.n)],
                                inc=cV)
                            pd_free[r] = cV.n
                            yadd_mark[(isb, hc, tg)] = cV.n
                    mm_done[isb - 1] = cT.n
                mm_done[isb] = cT.n
            r6f = r6.bitcast(F32)
            r6b = r6
            t2 = t0 // 2

            RG = T_CORE // 2

            def stg_ap(hc):
                p0 = ((hc // 4) % 2) * 8 + 2 * (hc % 4)
                return [xnt_f[:, p0 * RG + t2:p0 * RG + t2 + TH // 2],
                        xnt_f[:, (p0 + 1) * RG + t2:(p0 + 1) * RG + t2 + TH // 2]]
            xload = {}
            batch_add = {}
            FW = FC * T_CORE // 2

            def emit_xload(g):
                base = ((g % 2) * 8) * RG + t2
                w = [(cT, gu_mark[NISB - 1])] if g == 0 else []
                if g >= 2:
                    w.append((cV, batch_add[g - 2]))
                for q in range(2):
                    dst = bass.AP(xnt_f, base + q * RG,
                                  [[FW, 128], [T_CORE, 4], [1, TH // 2]])
                    s4 = xsrc[4 * g * 128:(4 * g + 4) * 128,
                              t0 + q * (TH // 2):t0 + (q + 1) * (TH // 2)]
                    dma("P", dst,
                        s4.rearrange("(blk p) c -> p blk c", p=128),
                        sg, waits=w if q == 0 else [], ng=(q == 0))
                xload[g] = sg.n
            emit_xload(0)
            emit_xload(1)
            xstore = {}
            silu_last = silu_mark[NISB * 2 - 1]
            stt_last = stt_mark[NISB * 2 - 1]
            stat_mm = {}
            addm = 0
            for hc in range(16):
                b = hc % 2
                xrow = yacc[:, hc * TH:hc * TH + TH]
                ysl = pC[:, b * 1024:b * 1024 + TH]
                rec.emit("V", lambda e, hc=hc, ysl=ysl: e.tensor_scalar_mul(
                    ysl, yacc[:, hc * TH:hc * TH + TH],
                    r5[:, DSCH + hc:DSCH + hc + 1]),
                    waits=[(cV, yadd_mark[(NISB - 1, hc, ntg - 1)]),
                           (cV, cV.n)], inc=cV)
                ys = cV.n
                sa, sb_ = stg_ap(hc)
                rec.emit("V", lambda e, xrow=xrow, sa=sa, ysl=ysl: e.tensor_tensor(
                    xrow[:, 0:TH // 2], sa, ysl[:, 0:TH // 2], OP.add),
                    waits=[(sg, xload[hc // 4]), (cV, ys)], inc=cV)
                rec.emit("V", lambda e, xrow=xrow, sb_=sb_, ysl=ysl: e.tensor_tensor(
                    xrow[:, TH // 2:TH], sb_, ysl[:, TH // 2:TH], OP.add),
                    waits=[(cV, cV.n)], inc=cV)
                addm = cV.n
                if hc % 4 == 3:
                    batch_add[hc // 4] = addm
                    if hc // 4 + 2 < 4:
                        emit_xload(hc // 4 + 2)
                if hc % 4 == 3:
                    dma("A", xdst.ap().rearrange("(blk p) c -> p blk c", p=128)
                        [:, hc - 3:hc + 1, t0:t0 + TH],
                        yacc[:, (hc - 3) * TH:(hc + 1) * TH].rearrange(
                            "p (blk c) -> p blk c", c=TH), st,
                        waits=[(cV, addm)])
                for q in range(1):
                    xstore[hc] = st.n
                if fuse_ln:
                    rec.emit("A", lambda e, xrow=xrow: e.activation(
                        out=r6b[:, 2048:2048 + TH], in_=xrow, func=AF.Copy),
                        waits=[(cV, addm), (cT, stat_mm.get(hc - 1, 0))], inc=cA)
                    rec.emit("A", lambda e: e.activation(
                        out=r6b[:, 3072:3072 + TH], in_=r6b[:, 2048:2048 + TH],
                        func=AF.Square), waits=[(cA, cA.n)], inc=cA)
                    sqh = cA.n
                    for tg in range(ntg):
                        s0 = tg * 512
                        rec.emit("T", lambda e, s0=s0, hc=hc: e.matmul(
                            pA[0:1, s0:s0 + 512], onesb[:, 0:1],
                            r6b[:, 2048 + s0:2048 + s0 + 512],
                            start=(hc == 0), stop=(hc == 15)),
                            waits=[(cA, sqh)] +
                                  ([(cA, silu_last), (cV, stt_last)]
                                   if (hc == 0 and tg == 0) else []))
                        rec.emit("T", lambda e, s0=s0, hc=hc: e.matmul(
                            pB[0:1, s0:s0 + 512], onesb[:, 0:1],
                            r6b[:, 3072 + s0:3072 + s0 + 512],
                            start=(hc == 0), stop=(hc == 15)),
                            inc=cT if tg == ntg - 1 else None)
                    stat_mm[hc] = cT.n
            if not fuse_ln:
                return st.n, 0
            stat_done = cT.n
            bc_mark = {}
            for tg in range(ntg):
                s0 = tg * 512
                w0 = [(cT, stat_done)]
                if tg > 0:
                    w0 += [(cT, bc_mark[tg - 1]), (cV, cV.n)]
                rec.emit("V", lambda e, s0=s0: e.tensor_scalar_mul(
                    r5[0:1, LNM:LNM + 512], pA[0:1, s0:s0 + 512], 1.0 / D),
                    waits=w0, inc=cV)
                rec.emit("V", lambda e, s0=s0: e.tensor_scalar_mul(
                    r5[0:1, LNV:LNV + 512], pB[0:1, s0:s0 + 512], 1.0 / D), inc=cV)
                rec.emit("V", lambda e, s0=s0: e.tensor_tensor(
                    pA[0:1, s0:s0 + 512], r5[0:1, LNM:LNM + 512],
                    r5[0:1, LNM:LNM + 512], OP.mult), waits=[(cV, cV.n)], inc=cV)
                rec.emit("V", lambda e, s0=s0: e.tensor_tensor(
                    r5[0:1, LNV:LNV + 512], r5[0:1, LNV:LNV + 512],
                    pA[0:1, s0:s0 + 512], OP.subtract), waits=[(cV, cV.n)], inc=cV)
                vdone = cV.n
                rec.emit("T", lambda e, s0=s0: e.matmul(
                    pA[:, s0:s0 + 512], onesrow[0:1, 0:128], r5[0:1, LNM:LNM + 512],
                    start=True, stop=True), waits=[(cV, vdone)], inc=cT)
                mb_mark = cT.n
                rec.emit("A", lambda e: e.activation(
                    out=r5[0:1, LNM:LNM + 512], in_=r5[0:1, LNV:LNV + 512],
                    func=AF.Sqrt, bias=r5[0:1, EPSs:EPSs + 1]),
                    waits=[(cV, vdone), (cT, mb_mark)], inc=cA)
                rec.emit("V", lambda e: e.reciprocal(
                    r5[0:1, LNM:LNM + 512], r5[0:1, LNM:LNM + 512]),
                    waits=[(cA, cA.n)], inc=cV)
                rec.emit("T", lambda e, s0=s0: e.matmul(
                    pB[:, s0:s0 + 512], onesrow[0:1, 0:128], r5[0:1, LNM:LNM + 512],
                    start=True, stop=True), waits=[(cV, cV.n)], inc=cT)
                bc_mark[tg] = cT.n
            for fc in range(16):
                xrow = yacc[:, fc * TH:fc * TH + TH]
                for tg in range(ntg):
                    s0 = tg * 512
                    rec.emit("V", lambda e, xrow=xrow, s0=s0: e.tensor_tensor(
                        xrow[:, s0:s0 + 512], xrow[:, s0:s0 + 512],
                        pA[:, s0:s0 + 512], OP.subtract),
                        waits=[(st, xstore[(fc // 4) * 4 + 3]),
                               (cT, bc_mark[ntg - 1])]
                        if tg == 0 else [(cV, cV.n)], inc=cV)
                    rec.emit("V", lambda e, xrow=xrow, s0=s0, fc=fc: e.tensor_tensor(
                        xnt[:, fc * T_CORE + t0 + s0: fc * T_CORE + t0 + s0 + 512],
                        xrow[:, s0:s0 + 512], pB[:, s0:s0 + 512], OP.mult),
                        waits=[(cV, cV.n)], inc=cV)
            return st.n, cV.n

        def attention(xn_mark):
            ntg4 = T_CORE // 512
            wqb = [r3[:, WQB:WQB + 2048], r3[:, WQB + 2048:WQB + 4096]]
            kbst = r3f[:, QB_F:QB_F + T_CORE]
            swp = r3f[:, SWP_F:SWP_F + T_CORE]
            klocT = r6[:, 0:T_CORE]
            vlocT = r6[:, 2048:2048 + T_CORE]
            vstg = r3[:, VSTG:VSTG + T_CORE]
            cosq = r3f[:, TCOSQ:TCOSQ + T_CORE]
            sinq = r3f[:, TSINQ:TSINQ + T_CORE]
            cosk = r3f[:, TCOSK:TCOSK + T_CORE]
            sink = r3f[:, TSINK:TSINK + T_CORE]

            tw = [(cT, cT.n), (st, st.n), (cA, cA.n)]
            dma("P", cosq, cosq_in[:, :], sg, waits=tw)
            dma("P", sinq, sinq_in[:, :], sg)
            dma("P", cosk, cosk_in[:, :], sg)
            dma("P", sink, sink_in[:, :], sg)
            dma("P", wqb[0].rearrange("p (dc o) -> p dc o", o=128),
                wk.ap().rearrange("(dc p) o -> p dc o", p=128), swo,
                waits=[(cT, cT.n), (st, st.n), (cA, cA.n)])
            tabs = sg.n

            def rotary(dst_bf, src_f, cos_t, sin_t, vw):
                rec.emit("V", lambda e: e.tensor_copy(swp[0:64, :], src_f[64:128, :]),
                         waits=vw, inc=cV)
                rec.emit("V", lambda e: e.tensor_copy(swp[64:128, :], src_f[0:64, :]),
                         waits=[(cV, cV.n)], inc=cV)
                rec.emit("V", lambda e: e.tensor_tensor(src_f, src_f, cos_t, OP.mult),
                         waits=[(cV, cV.n)], inc=cV)
                rec.emit("V", lambda e: e.tensor_tensor(swp, swp, sin_t, OP.mult),
                         waits=[(cV, cV.n)], inc=cV)
                rec.emit("V", lambda e: e.tensor_tensor(dst_bf, src_f, swp, OP.add),
                         waits=[(cV, cV.n)], inc=cV)
                return cV.n

            for dc in range(16):
                for tg in range(ntg4):
                    s0 = tg * 512
                    rec.emit("T", lambda e, tg=tg, dc=dc, s0=s0: e.matmul(
                        pC[:, s0:s0 + 512],
                        wqb[0][:, dc * 128:dc * 128 + 128],
                        xnt[:, dc * T_CORE + s0:dc * T_CORE + s0 + 512],
                        start=(dc == 0), stop=(dc == 15)),
                        waits=[(sg, tabs), (swo, swo.n), (cV, xn_mark)]
                        if (tg == 0 and dc == 0) else [],
                        inc=cT if (dc == 15 and tg == ntg4 - 1) else None)
            kmm = cT.n
            rec.emit("V", lambda e: e.tensor_scalar_add(
                kbst, pC[:, 0:T_CORE], r5[:, BK:BK + 1]),
                waits=[(cT, kmm), (cV, cV.n)], inc=cV)
            kv_evac = cV.n
            krot = rotary(klocT, kbst, cosk, sink, [(cV, kv_evac)])
            dma("P", cc_ik[:, :], klocT, sg, waits=[(cV, krot)])
            kout = sg.n
            dma("P", wqb[1].rearrange("p (dc o) -> p dc o", o=128),
                wv.ap().rearrange("(dc p) o -> p dc o", p=128), swo)
            vw = swo.n
            for dc in range(16):
                for tg in range(ntg4):
                    s0 = tg * 512
                    rec.emit("T", lambda e, tg=tg, dc=dc, s0=s0: e.matmul(
                        pC[:, s0:s0 + 512],
                        wqb[1][:, dc * 128:dc * 128 + 128],
                        xnt[:, dc * T_CORE + s0:dc * T_CORE + s0 + 512],
                        start=(dc == 0), stop=(dc == 15)),
                        waits=[(swo, vw), (cV, kv_evac)]
                        if (tg == 0 and dc == 0) else [],
                        inc=cT if (dc == 15 and tg == ntg4 - 1) else None)
            vmm = cT.n
            rec.emit("V", lambda e: e.tensor_scalar_add(
                vlocT, pC[:, 0:T_CORE], r5[:, BV:BV + 1]),
                waits=[(cT, vmm), (cV, cV.n)], inc=cV)
            vev = cV.n
            ntt = T_CORE // 128
            tp_done = {}
            for tt in range(ntt):
                pb = pAb if tt % 2 == 0 else pBb
                rec.emit("T", lambda e, tt=tt, pb=pb: e.transpose(
                    pb[:, 0:128], vlocT[:, tt * 128:tt * 128 + 128], ident[:, :]),
                    waits=[(cV, vev) if tt < 2 else (cA, tp_done.get(tt - 2, 0))],
                    inc=cT)
                tpm = cT.n
                rec.emit("A", lambda e, tt=tt, pb=pb: e.activation(
                    out=vstg[:, tt * 128:tt * 128 + 128], in_=pb[:, 0:128],
                    func=AF.Copy), waits=[(cT, tpm)], inc=cA)
                tp_done[tt] = cA.n
            dma("P", cc_iv.ap().rearrange("(tt p) d -> p tt d", p=128),
                vstg.rearrange("p (tt d) -> p tt d", d=128), sg,
                waits=[(cA, cA.n)])
            vout = sg.n
            grp = [[0, 1, 2, 3], [4, 5, 6, 7]]
            rec.emit("P", lambda e: e.collective_compute(
                "AllGather", OP.bypass, replica_groups=grp,
                ins=[cc_ik.ap().opt()], outs=[cc_ok.ap().opt()]),
                waits=[(sg, kout)], inc=scc)
            rec.emit("P", lambda e: e.collective_compute(
                "AllGather", OP.bypass, replica_groups=grp,
                ins=[cc_iv.ap().opt()], outs=[cc_ov.ap().opt()]),
                waits=[(sg, vout)], inc=scc)
            cc_done = scc.n
            for r in range(4):
                dma("S", r3[:, KK + r * T_CORE:KK + (r + 1) * T_CORE],
                    cc_ok[r * 128:(r + 1) * 128, :], sw,
                    waits=[(scc, cc_done - 1), (cV, krot)] if r == 0 else [],
                    ng=(r == 0))
            kload = sw.n

            qmm_mark = {}
            prev_qev = cV.n
            nch = KEYS // 512
            kms = {}
            red = {}
            rec.emit("V", lambda e: e.memset(r5[0:1, KAP + 1:KAP + 2], 0.0),
                     waits=[(cV, cV.n)], inc=cV)
            kmax_chain = cV.n
            for h in range(NH):
                b = h % 2
                for ch in ([2 * (h - 8), 2 * (h - 8) + 1]
                           if (h >= 8 and 2 * (h - 8) + 1 < nch) else []):
                    kb = ch % 2
                    rec.emit("A", lambda e, ch=ch, kb=kb: e.activation(
                        out=pstg[:, kb * 512:kb * 512 + 512],
                        in_=r3[:, KK + ch * 512:KK + ch * 512 + 512],
                        func=AF.Square),
                        waits=[(sw, kload), (cT, kms.get(ch - 2, 0))], inc=cA)
                    sqch = cA.n
                    kp = pB if kb == 0 else pA
                    rec.emit("T", lambda e, kb=kb, kp=kp: e.matmul(
                        kp[0:1, 512:1024],
                        onesb[:, 0:1], pstg[:, kb * 512:kb * 512 + 512],
                        start=True, stop=True),
                        waits=[(cA, sqch), (cV, red.get(ch - 2, kmax_chain))],
                        inc=cT)
                    kms[ch] = cT.n
                    rec.emit("V", lambda e, kp=kp: e.reduce_max(
                        r5[0:1, TMPS + 2:TMPS + 3],
                        kp[0:1, 512:1024],
                        axis=mybir.AxisListType.X),
                        waits=[(cT, kms[ch]), (cV, cV.n)], inc=cV)
                    rec.emit("V", lambda e: e.tensor_tensor(
                        r5[0:1, KAP + 1:KAP + 2], r5[0:1, KAP + 1:KAP + 2],
                        r5[0:1, TMPS + 2:TMPS + 3], OP.max),
                        waits=[(cV, cV.n)], inc=cV)
                    red[ch] = cV.n
                dma("P", wqb[b].rearrange("p (dc o) -> p dc o", o=128),
                    wq.ap().rearrange(
                    "(dc p) o -> p dc o", p=128)[:, :, h * 128:h * 128 + 128], swo,
                    waits=[(cT, qmm_mark.get(h - 2, vmm if h < 2 else 0)),
                           (sg, vout if h == 0 else 0)])
                wql = swo.n
                for dc in range(16):
                    for tg in range(ntg4):
                        s0 = tg * 512
                        rec.emit("T", lambda e, b=b, dc=dc, s0=s0: e.matmul(
                            pC[:, s0:s0 + 512],
                            wqb[b][:, dc * 128:dc * 128 + 128],
                            xnt[:, dc * T_CORE + s0:dc * T_CORE + s0 + 512],
                            start=(dc == 0), stop=(dc == 15)),
                            waits=[(swo, wql), (cV, prev_qev)]
                            if (tg == 0 and dc == 0) else [],
                            inc=cT if (dc == 15 and tg == ntg4 - 1) else None)
                qmm_mark[h] = cT.n
                rec.emit("V", lambda e, h=h: e.tensor_scalar_add(
                    kbst, pC[:, 0:T_CORE], r5[:, BQ + h:BQ + h + 1]),
                    waits=[(cT, qmm_mark[h]), (cV, cV.n)], inc=cV)
                qev = cV.n
                prev_qev = qev
                rec.emit("A", lambda e: e.activation(
                    out=r4.bitcast(BF16)[:, 0:T_CORE], in_=kbst, func=AF.Square),
                    waits=[(cV, qev), (cT, cT.n)], inc=cA)
                sqd = cA.n
                vfree = cV.n
                qa = yacc_b[:, h * T_CORE:(h + 1) * T_CORE]
                qrot = rotary(qa, kbst, cosq, sinq, [(cV, qev), (cA, sqd)])
                rec.emit("V", lambda e, h=h: e.memset(
                    r5[0:1, R2M + h:R2M + h + 1], 0.0), waits=[(cV, vfree)], inc=cV)
                rmm = {}
                for j in range(ntg4):
                    p = pA if j % 2 == 0 else pB
                    rec.emit("T", lambda e, j=j, p=p: e.matmul(
                        p[0:1, 0:512], onesb[:, 0:1],
                        r4.bitcast(BF16)[:, j * 512:j * 512 + 512],
                        start=True, stop=True),
                        waits=[(cA, sqd), (cV, rmm.get(j - 2, cV.n))], inc=cT)
                    rj = cT.n
                    rec.emit("V", lambda e, p=p: e.reduce_max(
                        r5[0:1, TMPS:TMPS + 1], p[0:1, 0:512],
                        axis=mybir.AxisListType.X),
                        waits=[(cT, rj), (cV, cV.n)], inc=cV)
                    rec.emit("V", lambda e, h=h: e.tensor_tensor(
                        r5[0:1, R2M + h:R2M + h + 1], r5[0:1, R2M + h:R2M + h + 1],
                        r5[0:1, TMPS:TMPS + 1], OP.max), waits=[(cV, cV.n)], inc=cV)
                    rmm[j] = cV.n
            qphase_done = (cT.n, cA.n, cV.n)

            kw = [(scc, cc_done), (cV, qphase_done[2]), (cT, qphase_done[0])]
            dma("S", r3[:, VV:VV + NKT * 128].rearrange("p (kt d) -> p kt d", d=128),
                cc_ov.ap().rearrange("(kt p) d -> p kt d", p=128), sw, waits=kw)
            kvload = sw.n
            rec.emit("V", lambda e: e.memset(r5[0:1, KAP:KAP + 1], 0.0), inc=cV)
            rec.emit("A", lambda e: e.activation(
                out=r5[0:1, KAP:KAP + 1], in_=r5[0:1, KAP + 1:KAP + 2],
                func=AF.Sqrt), waits=[(cV, cV.n)], inc=cA)
            kapd = cA.n
            rec.emit("A", lambda e: e.activation(
                out=r5[0:1, TMPS:TMPS + 16], in_=r5[0:1, R2M:R2M + 16],
                func=AF.Sqrt, scale=1.0 / HD), waits=[(cA, kapd), (cV, cV.n)], inc=cA)
            rec.emit("V", lambda e: e.tensor_scalar_mul(
                r5[0:1, TMPS:TMPS + 16], r5[0:1, TMPS:TMPS + 16],
                r5[0:1, KAP:KAP + 1]), waits=[(cA, cA.n)], inc=cV)
            rec.emit("T", lambda e: e.matmul(
                pA[:, 0:16], onesrow[0:1, 0:128], r5[0:1, TMPS:TMPS + 16],
                start=True, stop=True), waits=[(cV, cV.n)], inc=cT)
            rec.emit("V", lambda e: e.tensor_scalar_mul(
                r5[:, MHN:MHN + 16], pA[:, 0:16], -1.0),
                waits=[(cT, cT.n)], inc=cV)
            mh_done = cV.n

            u = T_CORE // 2
            lacc8 = xnt[:, 0:T_CORE]
            lacc32 = xnt_f[:, u:u + T_CORE]
            linv = xnt_f[0:1, u + T_CORE:u + 2 * T_CORE]
            ctxs = xnt[:, 5 * T_CORE:6 * T_CORE]
            pring = [r3[:, PRING + i * QH:PRING + (i + 1) * QH] for i in range(8)]
            ctx_evac = {-1: (cA.n, cV.n)}
            prev_exp63 = {}
            prev_tail = {}
            for h in range(NH):
                qa = yacc_b[:, h * T_CORE:(h + 1) * T_CORE]
                exp_mark = {}
                pv_mark = {}
                l_mark = {}
                for qh in range(2):
                    ps = [pA, pB]
                    cslice = pC[:, qh * QH:qh * QH + QH]
                    for kt in range(NKT):
                        tw = []
                        if kt == 0 and qh == 0:
                            tw += [(sw, kvload), (cV, mh_done)]
                        if h > 0 and kt < 2 and qh == 0:
                            tw.append((cA, prev_exp63[(NKT - 2 + kt, qh)]))
                        if kt < 2 and qh == 1:
                            tw.append((cA, exp_mark[(NKT - 2 + kt, 0)]))
                        if kt >= 2:
                            tw.append((cA, exp_mark[(kt - 2, qh)]))
                        p = ps[kt % 2]
                        for j in range(QH // 512):
                            rec.emit("T", lambda e, kt=kt, j=j, p=p, qa=qa, qh=qh: e.matmul(
                                p[:, j * 512:j * 512 + 512],
                                r3[:, KK + kt * 128:KK + kt * 128 + 128],
                                qa[:, qh * QH + j * 512:qh * QH + j * 512 + 512],
                                start=True, stop=True),
                                waits=tw if j == 0 else [],
                                inc=cT if j == (QH // 512 - 1) else None)
                        sc = cT.n
                        if kt > 0:
                            slot = ((kt - 1) % 4) * 2 + qh
                            pvw = [(cA, exp_mark[(kt - 1, qh)])]
                            if kt == 1 and qh == 0:
                                pvw.append((cV, ctx_evac[h - 1][1]))
                            for j in range(QH // 512):
                                rec.emit("T", lambda e, kt=kt, j=j, slot=slot, cslice=cslice: e.matmul(
                                    cslice[:, j * 512:j * 512 + 512],
                                    r3[:, VV + (kt - 1) * 128:VV + (kt - 1) * 128 + 128],
                                    pring[slot][:, j * 512:j * 512 + 512],
                                    start=(kt - 1 == 0), stop=False),
                                    waits=pvw if j == 0 else [],
                                    inc=cT if j == (QH // 512 - 1) else None)
                            pv_mark[(kt - 1, qh)] = cT.n
                        slot = (kt % 4) * 2 + qh
                        ew = [(cT, sc), (cT, pv_mark.get((kt - 4, qh), 0)),
                              (cV, l_mark.get((kt - 4, qh), 0))]
                        if kt < 4 and h > 0:
                            ew += [(cT, prev_tail[(kt, qh)][0]),
                                   (cV, prev_tail[(kt, qh)][1])]
                        rec.emit("A", lambda e, h=h, p=p, slot=slot: e.activation(
                            out=pring[slot][:, 0:QH], in_=p[:, 0:QH], func=AF.Exp,
                            bias=r5[:, MHN + h:MHN + h + 1]),
                            waits=ew, inc=cA)
                        exp_mark[(kt, qh)] = cA.n
                        lw = [(cA, exp_mark[(kt, qh)])]
                        l8 = lacc8[:, qh * QH:qh * QH + QH]
                        if kt % 8 == 0:
                            rec.emit("V", lambda e, slot=slot, l8=l8: e.tensor_copy(
                                l8, pring[slot][:, 0:QH]),
                                waits=lw + [(cV, cV.n)], inc=cV)
                        else:
                            rec.emit("V", lambda e, slot=slot, l8=l8: e.tensor_tensor(
                                l8, l8, pring[slot][:, 0:QH], OP.add),
                                waits=lw + [(cV, cV.n)], inc=cV)
                        if kt % 8 == 7 or kt == NKT - 1:
                            l32 = lacc32[:, qh * QH:qh * QH + QH]
                            if kt < 8:
                                rec.emit("V", lambda e, l32=l32, l8=l8: e.tensor_copy(
                                    l32, l8), waits=[(cV, cV.n)], inc=cV)
                            else:
                                rec.emit("V", lambda e, l32=l32, l8=l8: e.tensor_tensor(
                                    l32, l32, l8, OP.add), waits=[(cV, cV.n)], inc=cV)
                        l_mark[(kt, qh)] = cV.n
                    slot = ((NKT - 1) % 4) * 2 + qh
                    for j in range(QH // 512):
                        rec.emit("T", lambda e, j=j, slot=slot, cslice=cslice: e.matmul(
                            cslice[:, j * 512:j * 512 + 512],
                            r3[:, VV + (NKT - 1) * 128:VV + (NKT - 1) * 128 + 128],
                            pring[slot][:, j * 512:j * 512 + 512],
                            start=(NKT == 1), stop=True),
                            waits=[(cA, exp_mark[(NKT - 1, qh)])] if j == 0 else [],
                            inc=cT if j == (QH // 512 - 1) else None)
                    pv_mark[(NKT - 1, qh)] = cT.n
                pv_all = cT.n
                lfold = cV.n
                for j in range(T_CORE // 512):
                    rec.emit("A", lambda e, j=j: e.activation(
                        out=ctxs[:, j * 512:j * 512 + 512],
                        in_=pC[:, j * 512:j * 512 + 512], func=AF.Copy),
                        waits=[(cT, pv_all)] if j == 0 else [], inc=cA)
                    cevj = cA.n
                    rec.emit("T", lambda e, j=j: e.matmul(
                        pC[0:1, j * 512:j * 512 + 512], onesf[:, 0:1],
                        lacc32[:, j * 512:j * 512 + 512], start=True, stop=True),
                        waits=[(cA, cevj)] + ([(cV, lfold)] if j == 0 else []),
                        inc=cT)
                    lrj = cT.n
                    rec.emit("V", lambda e, j=j: e.reciprocal(
                        linv[0:1, j * 512:j * 512 + 512],
                        pC[0:1, j * 512:j * 512 + 512]),
                        waits=[(cT, lrj), (cV, cV.n)], inc=cV)
                    rcpj = cV.n
                    rec.emit("T", lambda e, j=j: e.matmul(
                        pC[:, j * 512:j * 512 + 512], onesrow[0:1, 0:128],
                        linv[0:1, j * 512:j * 512 + 512], start=True, stop=True),
                        waits=[(cV, rcpj)], inc=cT)
                    lbj = cT.n
                    rec.emit("V", lambda e, j=j, qa=qa: e.tensor_tensor(
                        qa[:, j * 512:j * 512 + 512], ctxs[:, j * 512:j * 512 + 512],
                        pC[:, j * 512:j * 512 + 512], OP.mult),
                        waits=[(cT, lbj), (cV, cV.n)], inc=cV)
                ctx_evac[h] = (cA.n, cV.n)
                prev_exp63 = dict(exp_mark)
                prev_tail = {(kt, qh): (pv_mark[(NKT - 4 + kt, qh)],
                                        l_mark[(NKT - 4 + kt, qh)])
                             for kt in range(4) for qh in range(2)}

            wob = [r3[:, WOB:WOB + 2048], r3[:, WOB + 2048:WOB + 4096]]
            u = T_CORE // 2
            xtC = [r4[:, 0:T_CORE], xnt_f[:, u + 3 * T_CORE:u + 4 * T_CORE]]
            o_f = xnt_f[:, u + 2 * T_CORE:u + 3 * T_CORE]
            ommT = {}
            ommV = {}
            ostore = {}
            for mo in range(NH):
                b = mo % 2
                dma("P", wob[b].rearrange("p (dc o) -> p dc o", o=128),
                    wo.ap().rearrange(
                    "(dc p) o -> p dc o", p=128)[:, :, mo * 128:mo * 128 + 128], swo,
                    waits=[(cT, ommT.get(mo - 2, 0)),
                           (cV, ctx_evac[NH - 1][1] if mo == 0 else 0)])
                wol = swo.n
                dma("P", xtC[b], xs[mo * 128:(mo + 1) * 128, 0:T_CORE], sg,
                    waits=[(st, ostore.get(mo - 2, 0))])
                xld = sg.n
                tw = [(swo, wol), (cV, ommV.get(mo - 1, cV.n if mo == 0 else 0))]
                for hc in range(16):
                    for j in range(T_CORE // 512):
                        rec.emit("T", lambda e, b=b, j=j, hc=hc: e.matmul(
                            pC[:, j * 512:j * 512 + 512],
                            wob[b][:, hc * 128:hc * 128 + 128],
                            yacc_b[:, hc * T_CORE + j * 512:hc * T_CORE + j * 512 + 512],
                            start=(hc == 0), stop=(hc == 15)),
                            waits=tw if (j == 0 and hc == 0) else [],
                            inc=cT if (j == T_CORE // 512 - 1 and hc == 15) else None)
                omm = cT.n
                ommT[mo] = omm
                rec.emit("V", lambda e, mo=mo: e.tensor_scalar_add(
                    o_f, pC[:, 0:T_CORE], r5[:, BO + mo:BO + mo + 1]),
                    waits=[(cT, omm), (cV, cV.n)], inc=cV)
                ommV[mo] = cV.n
                rec.emit("V", lambda e, b=b: e.tensor_tensor(
                    xtC[b], xtC[b], o_f, OP.add),
                    waits=[(sg, xld), (cV, ommV[mo])], inc=cV)
                dma("A", xs[mo * 128:(mo + 1) * 128, :], xtC[b], st,
                    waits=[(cV, cV.n)])
                ostore[mo] = st.n
            fw = [(st, ostore[fc]) for fc in range(16)]
            xn3a = ln_half(xs, 0, fc_waits=fw, bf16_stage=True,
                           dep_waits=[(cA, ctx_evac[NH - 1][0])])
            xn3b = ln_half(xs, TH, psA=pC[:, 0:1024],
                           psB=pC[:, 1024:2048], fc_waits=fw, bf16_stage=True,
                           dep_waits=[(cT, ommT[NH - 1]), (cV, ommV[NH - 1])])
            return xn3a, xn3b

        init_phase()
        prologue_casts()
        m = ln_half(x_in, 0, dep_waits=[(cV, cV.n), (cP, cP.n)])
        _, xn2a = ffn_half(1, 0, x_in, xs, m, direct=True, fuse_ln=(STAGE >= 2))
        m = ln_half(x_in, TH)
        _, xn2b = ffn_half(1, 1, x_in, xs, m, fuse_ln=(STAGE >= 2),
                           fresh=False)
        if STAGE >= 2:
            xn3a, xn3b = attention(max(xn2a, xn2b))
        else:
            xn3a = xn3b = 0
        if STAGE >= 3:
            ffn_half(3, 0, xs, out_fm, xn3a)
            ffn_half(3, 1, xs, out_fm, xn3b, fresh=False)
        else:
            for fc in range(FC):
                b = fc % 2
                dma("P", r4[:, b * 1024:b * 1024 + 1024],
                    xs[fc * 128:(fc + 1) * 128, 0:1024], sg,
                    waits=[(st, st.n)])
                dma("P", out_fm[fc * 128:(fc + 1) * 128, 0:1024],
                    r4[:, b * 1024:b * 1024 + 1024], sg)
                if T_CORE > 1024:
                    dma("P", r4[:, b * 1024:b * 1024 + 1024],
                        xs[fc * 128:(fc + 1) * 128, 1024:2048], sg)
                    dma("P", out_fm[fc * 128:(fc + 1) * 128, 1024:2048],
                        r4[:, b * 1024:b * 1024 + 1024], sg)

        rec.emit("A", None, waits=[(st, st.n)])
        rec.emit("P", None, waits=[(sg, sg.n), (st, st.n), (spc, spc.n),
                                   (swo, swo.n)])
        rec.emit("S", None, waits=[(sw, sw.n)])

        @block.tensor
        def _(t):
            for f in rec.ops["T"]:
                f(t)

        @block.scalar
        def _(a):
            for f in rec.ops["A"]:
                f(a)

        @block.vector
        def _(v):
            for f in rec.ops["V"]:
                f(v)

        @block.gpsimd
        def _(g):
            for f in rec.ops["P"]:
                f(g)

        @block.sync
        def _(s):
            for f in rec.ops["S"]:
                f(s)

    return nc


def make_core_inputs(inputs, core, T_CORE):
    b = core // 4
    q0 = (core % 4) * T_CORE
    bf = ml_dtypes.bfloat16
    x = np.asarray(inputs["L0_attn_x"], np.float32)
    x_fm = np.ascontiguousarray(x[b, q0:q0 + T_CORE, :].T)
    pos = (q0 + np.arange(T_CORE, dtype=np.float64))[None, :]
    dim = np.arange(64, dtype=np.float64)[:, None]
    ang = pos * (10000.0 ** (-2.0 * dim / 128.0))
    cos = np.cos(ang)
    sin = np.sin(ang)
    s = 1.0 / np.sqrt(128.0)
    cosq = (np.concatenate([cos, cos], 0) * s).astype(np.float32)
    sinq = (np.concatenate([-sin, sin], 0) * s).astype(np.float32)
    cosk = np.concatenate([cos, cos], 0).astype(np.float32)
    sink = np.concatenate([-sin, sin], 0).astype(np.float32)
    d = {
        "x_fm": x_fm,
        "gate_w": np.ascontiguousarray(inputs["L0_ffn_gate_w"], np.int32),
        "up_w": np.ascontiguousarray(inputs["L0_ffn_up_w"], np.int32),
        "down_w": np.ascontiguousarray(inputs["L0_ffn_down_w"], np.int32),
        "wq": np.ascontiguousarray(inputs["L0_attn_Wq"], np.float32),
        "wk": np.ascontiguousarray(inputs["L0_attn_Wk"], np.float32),
        "wv": np.ascontiguousarray(inputs["L0_attn_Wv"], np.float32),
        "wo": np.ascontiguousarray(inputs["L0_attn_Wo"], np.float32),
        "gsc": np.ascontiguousarray(
            np.asarray(inputs["L0_ffn_gate_scale"], np.float32).reshape(64, 128).T),
        "usc": np.ascontiguousarray(
            np.asarray(inputs["L0_ffn_up_scale"], np.float32).reshape(64, 128).T),
        "dsc": np.ascontiguousarray(
            np.asarray(inputs["L0_ffn_down_scale"], np.float32).reshape(16, 128).T),
        "bq": np.ascontiguousarray(
            np.asarray(inputs["L0_attn_bq"], np.float32).reshape(16, 128).T),
        "bk": np.asarray(inputs["L0_attn_bk"], np.float32).reshape(128, 1).copy(),
        "bv": np.asarray(inputs["L0_attn_bv"], np.float32).reshape(128, 1).copy(),
        "bo": np.ascontiguousarray(
            np.asarray(inputs["L0_attn_bo"], np.float32).reshape(16, 128).T),
        "cosq": cosq, "sinq": sinq, "cosk": cosk, "sink": sink,
        "ident": np.eye(128, dtype=bf),
    }
    return d


def kernel(**inputs):
    T_CORE = inputs["L0_attn_x"].shape[1] // 4
    nc = build(T_CORE)
    in_maps = [make_core_inputs(inputs, c, T_CORE) for c in range(8)]
    res = run_bass_kernel_spmd(nc, in_maps, core_ids=list(range(8)))
    B, T = inputs["L0_attn_x"].shape[0], inputs["L0_attn_x"].shape[1]
    out = np.empty((B, T, D), np.float32)
    for c in range(8):
        b = c // 4
        q0 = (c % 4) * T_CORE
        out[b, q0:q0 + T_CORE, :] = np.asarray(res.results[c]["out_fm"]).T
    return out


# version 15 (final)
# speedup vs baseline: 1.0156x; 1.0020x over previous
import numpy as np
import ml_dtypes
import concourse.bass as bass
import concourse.mybir as mybir
from concourse.bass_utils import run_bass_kernel_spmd

F32 = mybir.dt.float32
BF16 = mybir.dt.bfloat16
I32 = mybir.dt.int32
AF = mybir.ActivationFunctionType
OP = mybir.AluOpType

D = 2048
I_DIM = 8192
NH = 16
HD = 128
FC = D // 128
NISB = 32
ISB_I = I_DIM // NISB
EPS = 1e-5
STAGE = 3


class Sem:
    __slots__ = ("name", "n", "h", "base")

    def __init__(self, name):
        self.name = name
        self.n = 0
        self.base = 0
        self.h = None


class Rec:
    def __init__(self):
        self.ops = {k: [] for k in "TAVPS"}

    def emit(self, ek, fn, waits=(), inc=None, amt=1):
        ws = tuple((s, v) for (s, v) in waits)
        if inc is not None:
            inc.n += amt
        def run(eng, ws=ws, fn=fn, inc=inc, amt=amt):
            for s, v in ws:
                if v > 0:
                    eng.wait_ge(s.h, v)
            ins = fn(eng) if fn is not None else None
            if inc is not None:
                ins.then_inc(inc.h, amt)
        self.ops[ek].append(run)


def build(T_CORE):
    TH = T_CORE // 2
    NTG = TH // 512
    KEYS = 4 * T_CORE
    NKT = KEYS // 128
    QH = T_CORE // 2
    assert NTG >= 1 and QH <= 1024

    nc = bass.Bass("TRN2", target_bir_lowering=False, debug=False)

    x_in = nc.dram_tensor("x_fm", [D, T_CORE], F32, kind="ExternalInput")
    gate_w = nc.dram_tensor("gate_w", [I_DIM, D], I32, kind="ExternalInput")
    up_w = nc.dram_tensor("up_w", [I_DIM, D], I32, kind="ExternalInput")
    down_w = nc.dram_tensor("down_w", [D, I_DIM], I32, kind="ExternalInput")
    wq = nc.dram_tensor("wq", [D, D], F32, kind="ExternalInput")
    wk = nc.dram_tensor("wk", [D, HD], F32, kind="ExternalInput")
    wv = nc.dram_tensor("wv", [D, HD], F32, kind="ExternalInput")
    wo = nc.dram_tensor("wo", [D, D], F32, kind="ExternalInput")
    gsc_in = nc.dram_tensor("gsc", [128, 64], F32, kind="ExternalInput")
    usc_in = nc.dram_tensor("usc", [128, 64], F32, kind="ExternalInput")
    dsc_in = nc.dram_tensor("dsc", [128, 16], F32, kind="ExternalInput")
    bq_in = nc.dram_tensor("bq", [128, 16], F32, kind="ExternalInput")
    bk_in = nc.dram_tensor("bk", [128, 1], F32, kind="ExternalInput")
    bv_in = nc.dram_tensor("bv", [128, 1], F32, kind="ExternalInput")
    bo_in = nc.dram_tensor("bo", [128, 16], F32, kind="ExternalInput")
    cosq_in = nc.dram_tensor("cosq", [128, T_CORE], F32, kind="ExternalInput")
    sinq_in = nc.dram_tensor("sinq", [128, T_CORE], F32, kind="ExternalInput")
    cosk_in = nc.dram_tensor("cosk", [128, T_CORE], F32, kind="ExternalInput")
    sink_in = nc.dram_tensor("sink", [128, T_CORE], F32, kind="ExternalInput")
    ident_in = nc.dram_tensor("ident", [128, 128], BF16, kind="ExternalInput")

    out_fm = nc.dram_tensor("out_fm", [D, T_CORE], F32, kind="ExternalOutput")

    xs = nc.dram_tensor("xs", [D, T_CORE], F32)
    scr_g = nc.dram_tensor("scr_g", [I_DIM, D], BF16)
    scr_u = nc.dram_tensor("scr_u", [I_DIM, D], BF16)
    scr_d = nc.dram_tensor("scr_d", [NISB * D, ISB_I], BF16)
    scr2_g = nc.dram_tensor("scr2_g", [NISB * 128, 16 * ISB_I], BF16)
    scr2_u = nc.dram_tensor("scr2_u", [NISB * 128, 16 * ISB_I], BF16)
    scr2_d = nc.dram_tensor("scr2_d", [NISB * 128, 2 * D], BF16)
    cc_ik = nc.dram_tensor("cc_ik", [128, T_CORE], BF16)
    cc_ok = nc.dram_tensor("cc_ok", [512, T_CORE], BF16)
    cc_iv = nc.dram_tensor("cc_iv", [T_CORE, 128], BF16)
    cc_ov = nc.dram_tensor("cc_ov", [4 * T_CORE, 128], BF16)

    rec = Rec()
    sw, sg, st, scc = Sem("sw"), Sem("sg"), Sem("st"), Sem("scc")
    spc, swo, std, swd = Sem("spc"), Sem("swo"), Sem("std"), Sem("swd")
    cT, cA, cV, cP = Sem("cT"), Sem("cA"), Sem("cV"), Sem("cP")
    sems = [sw, sg, st, scc, spc, swo, std, swd, cT, cA, cV, cP]

    from contextlib import ExitStack
    with ExitStack() as ctx:
        xnt = ctx.enter_context(nc.sbuf_tensor("xnt", [128, FC * T_CORE], BF16))
        yacc = ctx.enter_context(nc.sbuf_tensor("yacc", [128, 16 * TH], F32))
        r3 = ctx.enter_context(nc.sbuf_tensor("r3", [128, 28672], BF16))
        r4 = ctx.enter_context(nc.sbuf_tensor("r4", [128, 2048], F32))
        r5 = ctx.enter_context(nc.sbuf_tensor("r5", [128, 1280], F32))
        r6 = ctx.enter_context(nc.sbuf_tensor("r6", [128, 4096], BF16))
        onesb = ctx.enter_context(nc.sbuf_tensor("onesb", [128, 2], BF16))
        onesf = ctx.enter_context(nc.sbuf_tensor("onesf", [128, 1], F32))
        onesrow = ctx.enter_context(nc.sbuf_tensor("onesrow", [1, 128], F32))
        ident = ctx.enter_context(nc.sbuf_tensor("ident_sb", [128, 128], BF16))
        pstg = ctx.enter_context(nc.sbuf_tensor("pstg", [128, 1024], BF16))
        pA = ctx.enter_context(nc.psum_tensor("pA", [128, 1024], F32))
        pB = ctx.enter_context(nc.psum_tensor("pB", [128, 1024], F32))
        pC = ctx.enter_context(nc.psum_tensor("pC", [128, 2048], F32))
        for s in sems:
            s.h = ctx.enter_context(nc.semaphore(s.name))
        block = ctx.enter_context(nc.Block())

        xnt_f = xnt.bitcast(F32)
        yacc_b = yacc.bitcast(BF16)
        r3f = r3.bitcast(F32)
        pAb = pA.bitcast(BF16)
        pBb = pB.bitcast(BF16)

        GSC, USC, DSC, DSCH = 0, 64, 128, 144
        BQ, BO, BK, BV = 160, 176, 192, 193
        EPSs, ONESF_UNUSED, MHN, R2M, KAP, TMPS = 194, 195, 196, 212, 228, 229
        LNM, LNV = 256, 768

        WG, WU, WD, HDN = 0, 8192, 16384, 24576
        KK, VV, WOB, PRING, CTXS = 8192, 0, 16384, 20480, 24576
        TCOSQ, TSINQ, TCOSK, TSINK = 0, 2048, 4096, 6144
        QB_F, SWP_F = 10240, 12288
        WQB = 16384
        VSTG = 20480

        def dma(ek, out, in_, sem, waits=(), ng=True, **kw):
            if ng:
                sem.base = sem.n
            waits = list(waits) + [(sem, sem.base)]
            rec.emit(ek, lambda e, o=out, i=in_, kw=kw: e.dma_start(out=o, in_=i, **kw),
                     waits=waits, inc=sem, amt=16)

        def init_phase():
            dma("P", r5[:, GSC:GSC + 64], gsc_in[:, :], sg)
            dma("P", r5[:, USC:USC + 64], usc_in[:, :], sg, ng=False)
            dma("P", r5[:, DSC:DSC + 16], dsc_in[:, :], sg, ng=False)
            dma("P", r5[:, BQ:BQ + 16], bq_in[:, :], sg, ng=False)
            dma("P", r5[:, BK:BK + 1], bk_in[:, :], sg, ng=False)
            dma("P", r5[:, BV:BV + 1], bv_in[:, :], sg, ng=False)
            dma("P", r5[:, BO:BO + 16], bo_in[:, :], sg, ng=False)
            dma("P", ident[:, :], ident_in[:, :], sg, ng=False)
            rec.emit("P", lambda e: e.memset(onesb[:, :], 1.0), inc=cP)
            rec.emit("P", lambda e: e.memset(onesf[:, :], 1.0), inc=cP)
            rec.emit("P", lambda e: e.memset(onesrow[:, :], 1.0), inc=cP)
            rec.emit("P", lambda e: e.memset(r5[:, EPSs:EPSs + 1], EPS), inc=cP)
            rec.emit("V", lambda e: e.tensor_scalar_mul(
                r5[:, DSCH:DSCH + 16], r5[:, DSC:DSC + 16], 0.5),
                waits=[(sg, sg.n)], inc=cV)

        cast_mark = {}

        def prologue_casts():
            for isb in range(NISB):
                r0 = isb * ISB_I
                dma("P", scr_g[r0:r0 + ISB_I, :], gate_w[r0:r0 + ISB_I, :], spc)
                dma("P", scr_u[r0:r0 + ISB_I, :], up_w[r0:r0 + ISB_I, :], spc, ng=False)
                dma("P", scr_d[isb * D:(isb + 1) * D, :], down_w[:, r0:r0 + ISB_I],
                    spc, ng=False)
                cast_mark[isb] = spc.n

        def ln_half(src, t0, dep_waits=(), stage=None, psA=None, psB=None,
                    fc_waits=None, bf16_stage=False):
            ntg = TH // 512
            e_cV, e_cT, e_st = cV.n, cT.n, st.n
            xbf = [r6[:, 0:TH], r6[:, 1024:1024 + TH]]
            sqb = [r6[:, 2048:2048 + TH], r6[:, 3072:3072 + TH]]
            xst = stage if stage is not None else [r4[:, 0:TH], r4[:, 1024:1024 + TH]]
            if psA is None:
                psA, psB = pA, pB
            mm_done = {}
            for fc in range(FC):
                b = fc % 2
                w = list(dep_waits) if fc < 2 else []
                if fc_waits is not None:
                    w.append(fc_waits[fc])
                if fc < 2:
                    w.append((cV, e_cV))
                    if fc_waits is None:
                        w.append((st, e_st))
                if fc >= 2:
                    w.append((cT, mm_done[fc - 2]))
                if bf16_stage:
                    dma("P", xbf[b], src[fc * 128:(fc + 1) * 128, t0:t0 + TH], sg,
                        waits=w + [(cT, mm_done.get(fc - 2, e_cT))])
                    rec.emit("A", lambda e, b=b: e.activation(
                        out=sqb[b], in_=xbf[b], func=AF.Square),
                        waits=[(sg, sg.n), (cA, cA.n),
                               (cV, e_cV if fc < 2 else 0)], inc=cA)
                else:
                    dma("P", xst[b], src[fc * 128:(fc + 1) * 128, t0:t0 + TH], sg,
                        waits=w)
                    rec.emit("A", lambda e, b=b: e.activation(
                        out=xbf[b], in_=xst[b], func=AF.Copy),
                        waits=[(sg, sg.n), (cT, mm_done.get(fc - 2, e_cT)),
                               (cV, e_cV if fc < 2 else 0)], inc=cA)
                    rec.emit("A", lambda e, b=b: e.activation(
                        out=sqb[b], in_=xbf[b], func=AF.Square),
                        waits=[(cA, cA.n)], inc=cA)
                aw = cA.n
                for tg in range(ntg):
                    rec.emit("T", lambda e, b=b, tg=tg, fc=fc: e.matmul(
                        psA[0:1, tg * 512:tg * 512 + 512], onesb[:, 0:1],
                        xbf[b][:, tg * 512:tg * 512 + 512],
                        start=(fc == 0), stop=(fc == FC - 1)),
                        waits=[(cA, aw)] if tg == 0 else [])
                    rec.emit("T", lambda e, b=b, tg=tg, fc=fc: e.matmul(
                        psB[0:1, tg * 512:tg * 512 + 512], onesb[:, 0:1],
                        sqb[b][:, tg * 512:tg * 512 + 512],
                        start=(fc == 0), stop=(fc == FC - 1)),
                        inc=cT if (tg == ntg - 1) else None)
                mm_done[fc] = cT.n
            stat_done = cT.n
            bc_mark = {}
            for tg in range(ntg):
                s0 = tg * 512
                w0 = [(cT, stat_done)]
                if tg > 0:
                    w0 += [(cT, bc_mark[tg - 1]), (cV, cV.n)]
                rec.emit("V", lambda e, s0=s0: e.tensor_scalar_mul(
                    r5[0:1, LNM:LNM + 512], psA[0:1, s0:s0 + 512], 1.0 / D),
                    waits=w0, inc=cV)
                rec.emit("V", lambda e, s0=s0: e.tensor_scalar_mul(
                    r5[0:1, LNV:LNV + 512], psB[0:1, s0:s0 + 512], 1.0 / D), inc=cV)
                rec.emit("V", lambda e, s0=s0: e.tensor_tensor(
                    psA[0:1, s0:s0 + 512], r5[0:1, LNM:LNM + 512],
                    r5[0:1, LNM:LNM + 512], OP.mult), waits=[(cV, cV.n)], inc=cV)
                rec.emit("V", lambda e, s0=s0: e.tensor_tensor(
                    r5[0:1, LNV:LNV + 512], r5[0:1, LNV:LNV + 512],
                    psA[0:1, s0:s0 + 512], OP.subtract), waits=[(cV, cV.n)], inc=cV)
                vdone = cV.n
                rec.emit("T", lambda e, s0=s0: e.matmul(
                    psA[:, s0:s0 + 512], onesrow[0:1, 0:128], r5[0:1, LNM:LNM + 512],
                    start=True, stop=True), waits=[(cV, vdone)], inc=cT)
                mb_mark = cT.n
                rec.emit("A", lambda e: e.activation(
                    out=r5[0:1, LNM:LNM + 512], in_=r5[0:1, LNV:LNV + 512],
                    func=AF.Sqrt, bias=r5[0:1, EPSs:EPSs + 1]),
                    waits=[(cV, vdone), (cT, mb_mark)], inc=cA)
                rec.emit("V", lambda e: e.reciprocal(
                    r5[0:1, LNM:LNM + 512], r5[0:1, LNM:LNM + 512]),
                    waits=[(cA, cA.n)], inc=cV)
                rec.emit("T", lambda e, s0=s0: e.matmul(
                    psB[:, s0:s0 + 512], onesrow[0:1, 0:128], r5[0:1, LNM:LNM + 512],
                    start=True, stop=True), waits=[(cV, cV.n)], inc=cT)
                bc_mark[tg] = cT.n
            e2_cT = cT.n
            st_dep = {}
            pf = pstg.bitcast(F32)
            for fc in range(FC):
                b = fc % 2
                w = [(cV, st_dep.get(fc - 2, 0))]
                if fc < 2:
                    w.append((cT, bc_mark[ntg - 1]))
                if bf16_stage:
                    dma("P", xbf[b], src[fc * 128:(fc + 1) * 128, t0:t0 + TH], sg,
                        waits=w)
                    ld = sg.n
                    for tg in range(ntg):
                        s0 = tg * 512
                        rec.emit("V", lambda e, b=b, s0=s0: e.tensor_tensor(
                            pf[:, 0:512], xbf[b][:, s0:s0 + 512],
                            psA[:, s0:s0 + 512], OP.subtract),
                            waits=[(sg, ld), (cT, e2_cT), (cV, cV.n)], inc=cV)
                        rec.emit("V", lambda e, s0=s0, fc=fc: e.tensor_tensor(
                            xnt[:, fc * T_CORE + t0 + s0: fc * T_CORE + t0 + s0 + 512],
                            pf[:, 0:512], psB[:, s0:s0 + 512], OP.mult),
                            waits=[(cV, cV.n)], inc=cV)
                else:
                    dma("P", xst[b], src[fc * 128:(fc + 1) * 128, t0:t0 + TH], sg,
                        waits=w)
                    ld = sg.n
                    for tg in range(ntg):
                        s0 = tg * 512
                        rec.emit("V", lambda e, b=b, s0=s0: e.tensor_tensor(
                            xst[b][:, s0:s0 + 512], xst[b][:, s0:s0 + 512],
                            psA[:, s0:s0 + 512], OP.subtract),
                            waits=[(sg, ld), (cT, e2_cT)], inc=cV)
                        rec.emit("V", lambda e, b=b, s0=s0, fc=fc: e.tensor_tensor(
                            xnt[:, fc * T_CORE + t0 + s0: fc * T_CORE + t0 + s0 + 512],
                            xst[b][:, s0:s0 + 512], psB[:, s0:s0 + 512], OP.mult),
                            waits=[(cV, cV.n)], inc=cV)
                st_dep[fc] = cV.n
            return cV.n

        def ffn_half(app, half, xsrc, xdst, xn_mark, direct=False, fuse_ln=False,
                     fresh=True):
            t0 = half * TH
            ntg = TH // 512
            e_cT, e_cV, e_cA, e_st = cT.n, cV.n, cA.n, st.n
            wgb = [r3[:, WG:WG + 4096], r3[:, WG + 4096:WG + 8192]]
            wub = [r3[:, WU:WU + 4096], r3[:, WU + 4096:WU + 8192]]
            wdb = [r3[:, WD:WD + 4096], r3[:, WD + 4096:WD + 8192]]
            hdn = [r3[:, HDN:HDN + 2048], r3[:, HDN + 2048:HDN + 4096]]
            sgb = r6[:, 0:TH]
            rec.emit("P", lambda e: e.memset(yacc[:, :], 0.0),
                     waits=[(cA, cA.n), (cV, cV.n), (st, st.n)], inc=cP)
            yinit = cP.n
            mm_done = {}
            gu_mark = {}
            w_ready = {}
            wd_ready = {}
            yadd_mark = {}
            pd_free = {}
            stt_mark = {}
            silu_mark = {}
            scr2w = {}
            for isb in range(NISB):
                buf = isb % 2
                war = [(cT, gu_mark.get(isb - 2, e_cT))]
                wdw = [(cT, mm_done.get(isb - 2, e_cT))]
                if isb < 2 and fresh:
                    war += [(cV, e_cV), (cA, e_cA)]
                    wdw += [(cV, e_cV), (cA, e_cA)]
                if direct:
                    war += [(spc, cast_mark[isb]), (st, scr2w.get(isb - 2, e_st))]
                    wdw += [(spc, cast_mark[isb]), (st, scr2w.get(isb - 2, e_st))]
                    for hc in range(16):
                        dma("S", wgb[buf][:, hc * 256:hc * 256 + 256],
                            scr_g[isb * 256:isb * 256 + 256, hc * 128:hc * 128 + 128],
                            sw, waits=war if hc == 0 else [],
                            ng=(hc == 0), transpose=True)
                        dma("S", wub[buf][:, hc * 256:hc * 256 + 256],
                            scr_u[isb * 256:isb * 256 + 256, hc * 128:hc * 128 + 128],
                            sw, ng=False, transpose=True)
                    ldm = sw.n
                    for wb in (wgb, wub):
                        rec.emit("A", lambda e, wb=wb, buf=buf: e.activation(
                            out=wb[buf], in_=wb[buf], func=AF.Copy, bias=-128.0),
                            waits=[(sw, ldm)], inc=cA)
                    negm = cA.n
                    for ic2 in range(2):
                        dma("S", wdb[buf][:, ic2 * 2048:ic2 * 2048 + 2048],
                            scr_d[isb * D:(isb + 1) * D, ic2 * 128:ic2 * 128 + 128],
                            swd, waits=wdw if ic2 == 0 else [],
                            ng=(ic2 == 0), transpose=True)
                    ldmd = swd.n
                    rec.emit("A", lambda e, buf=buf: e.activation(
                        out=wdb[buf], in_=wdb[buf], func=AF.Copy, bias=-128.0),
                        waits=[(swd, ldmd)], inc=cA)
                    negmd = cA.n
                    dma("A", scr2_g[isb * 128:(isb + 1) * 128, :], wgb[buf], st,
                        waits=[(cA, negm)])
                    dma("A", scr2_u[isb * 128:(isb + 1) * 128, :], wub[buf], st,
                        ng=False)
                    dma("A", scr2_d[isb * 128:(isb + 1) * 128, :], wdb[buf], st,
                        ng=False, waits=[(cA, negmd)])
                    scr2w[isb] = st.n
                    w_ready[isb] = (cA, negm)
                    wd_ready[isb] = (cA, negmd)
                else:
                    dma("S", wgb[buf], scr2_g[isb * 128:(isb + 1) * 128, :], sw,
                        waits=war)
                    dma("S", wub[buf], scr2_u[isb * 128:(isb + 1) * 128, :], sw,
                        ng=False)
                    w_ready[isb] = (sw, sw.n)
                    dma("S", wdb[buf], scr2_d[isb * 128:(isb + 1) * 128, :], swd,
                        waits=wdw)
                    wd_ready[isb] = (swd, swd.n)
                for ic2 in range(2):
                    ic = isb * 2 + ic2
                    tw = [w_ready[isb], (cA, silu_mark.get(ic - 1, 0)),
                          (cV, xn_mark)]
                    first = True
                    for hc in range(16):
                        for tg in range(ntg):
                            s0 = tg * 512
                            rec.emit("T", lambda e, buf=buf, hc=hc, ic2=ic2, s0=s0: e.matmul(
                                pA[:, s0:s0 + 512],
                                wgb[buf][:, hc * 256 + ic2 * 128:hc * 256 + ic2 * 128 + 128],
                                xnt[:, hc * T_CORE + t0 + s0: hc * T_CORE + t0 + s0 + 512],
                                start=(hc == 0), stop=(hc == 15)),
                                waits=tw if first else [],
                                inc=cT if (tg == ntg - 1 and hc == 15) else None)
                            first = False
                    g_done = cT.n
                    firstu = True
                    for hc in range(16):
                        for tg in range(ntg):
                            s0 = tg * 512
                            rec.emit("T", lambda e, buf=buf, hc=hc, ic2=ic2, s0=s0: e.matmul(
                                pB[:, s0:s0 + 512],
                                wub[buf][:, hc * 256 + ic2 * 128:hc * 256 + ic2 * 128 + 128],
                                xnt[:, hc * T_CORE + t0 + s0: hc * T_CORE + t0 + s0 + 512],
                                start=(hc == 0), stop=(hc == 15)),
                                waits=[(cV, stt_mark.get(ic - 1, 0))] if firstu else [],
                                inc=cT if (tg == ntg - 1 and hc == 15) else None)
                            firstu = False
                    u_done = cT.n
                    rec.emit("A", lambda e, ic=ic: e.activation(
                        out=sgb[:, 0:TH], in_=pA[:, 0:TH], func=AF.Silu,
                        scale=r5[:, GSC + ic:GSC + ic + 1]),
                        waits=[(cT, g_done), (cV, stt_mark.get(ic - 1, 0))], inc=cA)
                    silu_mark[ic] = cA.n
                    rec.emit("V", lambda e, buf=buf, ic2=ic2, ic=ic: e.scalar_tensor_tensor(
                        hdn[buf][:, ic2 * TH:ic2 * TH + TH], pB[:, 0:TH],
                        r5[:, USC + ic:USC + ic + 1], sgb[:, 0:TH],
                        OP.mult, OP.mult),
                        waits=[(cT, u_done), (cA, silu_mark[ic])], inc=cV)
                    stt_mark[ic] = cV.n
                gu_mark[isb] = cT.n
                if isb % 2 == 1:
                    for hc in range(16):
                        rs = [(hc * ntg + tg) % 4 for tg in range(ntg)]
                        tw = [(cV, stt_mark[isb * 2 + 1]),
                              wd_ready[isb - 1], wd_ready[isb]]
                        tw += [(cV, pd_free.get(r, 0)) for r in rs]
                        first = True
                        for sb in (isb - 1, isb):
                            bu = sb % 2
                            for ic2 in range(2):
                                last = (sb == isb and ic2 == 1)
                                for tg in range(ntg):
                                    s0 = tg * 512
                                    r = rs[tg]
                                    rec.emit("T", lambda e, bu=bu, hc=hc, ic2=ic2, s0=s0, r=r, sb=sb, first_g=(sb == isb - 1 and ic2 == 0), last=last: e.matmul(
                                        pC[:, r * 512:r * 512 + 512],
                                        wdb[bu][:, ic2 * 2048 + hc * 128:ic2 * 2048 + hc * 128 + 128],
                                        hdn[bu][:, ic2 * TH + s0:ic2 * TH + s0 + 512],
                                        start=first_g, stop=last),
                                        waits=tw if first else [],
                                        inc=cT if (last and tg == ntg - 1) else None)
                                    first = False
                        dn_done = cT.n
                        for tg in range(ntg):
                            s0 = tg * 512
                            r = rs[tg]
                            rec.emit("V", lambda e, hc=hc, s0=s0, r=r: e.tensor_tensor(
                                yacc[:, hc * TH + s0:hc * TH + s0 + 512],
                                yacc[:, hc * TH + s0:hc * TH + s0 + 512],
                                pC[:, r * 512:r * 512 + 512], OP.add),
                                waits=[(cT, dn_done), (cP, yinit)]
                                if tg == 0 else [(cV, cV.n)],
                                inc=cV)
                            pd_free[r] = cV.n
                            yadd_mark[(isb, hc, tg)] = cV.n
                    mm_done[isb - 1] = cT.n
                mm_done[isb] = cT.n
            r6f = r6.bitcast(F32)
            r6b = r6
            t2 = t0 // 2

            RG = T_CORE // 2

            def stg_ap(hc):
                p0 = ((hc // 4) % 2) * 8 + 2 * (hc % 4)
                return [xnt_f[:, p0 * RG + t2:p0 * RG + t2 + TH // 2],
                        xnt_f[:, (p0 + 1) * RG + t2:(p0 + 1) * RG + t2 + TH // 2]]
            xload = {}
            batch_add = {}
            FW = FC * T_CORE // 2

            def emit_xload(g):
                base = ((g % 2) * 8) * RG + t2
                w = [(cT, gu_mark[NISB - 1])] if g == 0 else []
                if g >= 2:
                    w.append((cV, batch_add[g - 2]))
                for q in range(2):
                    dst = bass.AP(xnt_f, base + q * RG,
                                  [[FW, 128], [T_CORE, 4], [1, TH // 2]])
                    s4 = xsrc[4 * g * 128:(4 * g + 4) * 128,
                              t0 + q * (TH // 2):t0 + (q + 1) * (TH // 2)]
                    dma("P", dst,
                        s4.rearrange("(blk p) c -> p blk c", p=128),
                        sg, waits=w if q == 0 else [], ng=(q == 0))
                xload[g] = sg.n
            emit_xload(0)
            emit_xload(1)
            xstore = {}
            silu_last = silu_mark[NISB * 2 - 1]
            stt_last = stt_mark[NISB * 2 - 1]
            stat_mm = {}
            addm = 0
            for hc in range(16):
                b = hc % 2
                xrow = yacc[:, hc * TH:hc * TH + TH]
                ysl = pC[:, b * 1024:b * 1024 + TH]
                rec.emit("V", lambda e, hc=hc, ysl=ysl: e.tensor_scalar_mul(
                    ysl, yacc[:, hc * TH:hc * TH + TH],
                    r5[:, DSCH + hc:DSCH + hc + 1]),
                    waits=[(cV, yadd_mark[(NISB - 1, hc, ntg - 1)]),
                           (cV, cV.n)], inc=cV)
                ys = cV.n
                sa, sb_ = stg_ap(hc)
                rec.emit("V", lambda e, xrow=xrow, sa=sa, ysl=ysl: e.tensor_tensor(
                    xrow[:, 0:TH // 2], sa, ysl[:, 0:TH // 2], OP.add),
                    waits=[(sg, xload[hc // 4]), (cV, ys)], inc=cV)
                rec.emit("V", lambda e, xrow=xrow, sb_=sb_, ysl=ysl: e.tensor_tensor(
                    xrow[:, TH // 2:TH], sb_, ysl[:, TH // 2:TH], OP.add),
                    waits=[(cV, cV.n)], inc=cV)
                addm = cV.n
                if hc % 4 == 3:
                    batch_add[hc // 4] = addm
                    if hc // 4 + 2 < 4:
                        emit_xload(hc // 4 + 2)
                if hc % 4 == 3:
                    dma("A", xdst.ap().rearrange("(blk p) c -> p blk c", p=128)
                        [:, hc - 3:hc + 1, t0:t0 + TH],
                        yacc[:, (hc - 3) * TH:(hc + 1) * TH].rearrange(
                            "p (blk c) -> p blk c", c=TH), st,
                        waits=[(cV, addm)])
                for q in range(1):
                    xstore[hc] = st.n
                if fuse_ln:
                    rec.emit("A", lambda e, xrow=xrow: e.activation(
                        out=r6b[:, 2048:2048 + TH], in_=xrow, func=AF.Copy),
                        waits=[(cV, addm), (cT, stat_mm.get(hc - 1, 0))], inc=cA)
                    rec.emit("A", lambda e: e.activation(
                        out=r6b[:, 3072:3072 + TH], in_=r6b[:, 2048:2048 + TH],
                        func=AF.Square), waits=[(cA, cA.n)], inc=cA)
                    sqh = cA.n
                    for tg in range(ntg):
                        s0 = tg * 512
                        rec.emit("T", lambda e, s0=s0, hc=hc: e.matmul(
                            pA[0:1, s0:s0 + 512], onesb[:, 0:1],
                            r6b[:, 2048 + s0:2048 + s0 + 512],
                            start=(hc == 0), stop=(hc == 15)),
                            waits=[(cA, sqh)] +
                                  ([(cA, silu_last), (cV, stt_last)]
                                   if (hc == 0 and tg == 0) else []))
                        rec.emit("T", lambda e, s0=s0, hc=hc: e.matmul(
                            pB[0:1, s0:s0 + 512], onesb[:, 0:1],
                            r6b[:, 3072 + s0:3072 + s0 + 512],
                            start=(hc == 0), stop=(hc == 15)),
                            inc=cT if tg == ntg - 1 else None)
                    stat_mm[hc] = cT.n
            if not fuse_ln:
                return st.n, 0
            stat_done = cT.n
            bc_mark = {}
            for tg in range(ntg):
                s0 = tg * 512
                w0 = [(cT, stat_done)]
                if tg > 0:
                    w0 += [(cT, bc_mark[tg - 1]), (cV, cV.n)]
                rec.emit("V", lambda e, s0=s0: e.tensor_scalar_mul(
                    r5[0:1, LNM:LNM + 512], pA[0:1, s0:s0 + 512], 1.0 / D),
                    waits=w0, inc=cV)
                rec.emit("V", lambda e, s0=s0: e.tensor_scalar_mul(
                    r5[0:1, LNV:LNV + 512], pB[0:1, s0:s0 + 512], 1.0 / D), inc=cV)
                rec.emit("V", lambda e, s0=s0: e.tensor_tensor(
                    pA[0:1, s0:s0 + 512], r5[0:1, LNM:LNM + 512],
                    r5[0:1, LNM:LNM + 512], OP.mult), waits=[(cV, cV.n)], inc=cV)
                rec.emit("V", lambda e, s0=s0: e.tensor_tensor(
                    r5[0:1, LNV:LNV + 512], r5[0:1, LNV:LNV + 512],
                    pA[0:1, s0:s0 + 512], OP.subtract), waits=[(cV, cV.n)], inc=cV)
                vdone = cV.n
                rec.emit("T", lambda e, s0=s0: e.matmul(
                    pA[:, s0:s0 + 512], onesrow[0:1, 0:128], r5[0:1, LNM:LNM + 512],
                    start=True, stop=True), waits=[(cV, vdone)], inc=cT)
                mb_mark = cT.n
                rec.emit("A", lambda e: e.activation(
                    out=r5[0:1, LNM:LNM + 512], in_=r5[0:1, LNV:LNV + 512],
                    func=AF.Sqrt, bias=r5[0:1, EPSs:EPSs + 1]),
                    waits=[(cV, vdone), (cT, mb_mark)], inc=cA)
                rec.emit("V", lambda e: e.reciprocal(
                    r5[0:1, LNM:LNM + 512], r5[0:1, LNM:LNM + 512]),
                    waits=[(cA, cA.n)], inc=cV)
                rec.emit("T", lambda e, s0=s0: e.matmul(
                    pB[:, s0:s0 + 512], onesrow[0:1, 0:128], r5[0:1, LNM:LNM + 512],
                    start=True, stop=True), waits=[(cV, cV.n)], inc=cT)
                bc_mark[tg] = cT.n
            for fc in range(16):
                xrow = yacc[:, fc * TH:fc * TH + TH]
                for tg in range(ntg):
                    s0 = tg * 512
                    rec.emit("V", lambda e, xrow=xrow, s0=s0: e.tensor_tensor(
                        xrow[:, s0:s0 + 512], xrow[:, s0:s0 + 512],
                        pA[:, s0:s0 + 512], OP.subtract),
                        waits=[(st, xstore[(fc // 4) * 4 + 3]),
                               (cT, bc_mark[ntg - 1])]
                        if tg == 0 else [(cV, cV.n)], inc=cV)
                    rec.emit("V", lambda e, xrow=xrow, s0=s0, fc=fc: e.tensor_tensor(
                        xnt[:, fc * T_CORE + t0 + s0: fc * T_CORE + t0 + s0 + 512],
                        xrow[:, s0:s0 + 512], pB[:, s0:s0 + 512], OP.mult),
                        waits=[(cV, cV.n)], inc=cV)
            return st.n, cV.n

        def attention(xn_mark):
            ntg4 = T_CORE // 512
            wqb = [r3[:, WQB:WQB + 2048], r3[:, WQB + 2048:WQB + 4096]]
            kbst = r3f[:, QB_F:QB_F + T_CORE]
            swp = r3f[:, SWP_F:SWP_F + T_CORE]
            klocT = r6[:, 0:T_CORE]
            vlocT = r6[:, 2048:2048 + T_CORE]
            vstg = r3[:, VSTG:VSTG + T_CORE]
            cosq = r3f[:, TCOSQ:TCOSQ + T_CORE]
            sinq = r3f[:, TSINQ:TSINQ + T_CORE]
            cosk = r3f[:, TCOSK:TCOSK + T_CORE]
            sink = r3f[:, TSINK:TSINK + T_CORE]

            tw = [(cT, cT.n), (st, st.n), (cA, cA.n)]
            dma("P", cosq, cosq_in[:, :], sg, waits=tw)
            dma("P", sinq, sinq_in[:, :], sg)
            dma("P", cosk, cosk_in[:, :], sg)
            dma("P", sink, sink_in[:, :], sg)
            dma("P", wqb[0].rearrange("p (dc o) -> p dc o", o=128),
                wk.ap().rearrange("(dc p) o -> p dc o", p=128), swo,
                waits=[(cT, cT.n), (st, st.n), (cA, cA.n)])
            tabs = sg.n

            def rotary(dst_bf, src_f, cos_t, sin_t, vw):
                rec.emit("V", lambda e: e.tensor_copy(swp[0:64, :], src_f[64:128, :]),
                         waits=vw, inc=cV)
                rec.emit("V", lambda e: e.tensor_copy(swp[64:128, :], src_f[0:64, :]),
                         waits=[(cV, cV.n)], inc=cV)
                rec.emit("V", lambda e: e.tensor_tensor(src_f, src_f, cos_t, OP.mult),
                         waits=[(cV, cV.n)], inc=cV)
                rec.emit("V", lambda e: e.tensor_tensor(swp, swp, sin_t, OP.mult),
                         waits=[(cV, cV.n)], inc=cV)
                rec.emit("V", lambda e: e.tensor_tensor(dst_bf, src_f, swp, OP.add),
                         waits=[(cV, cV.n)], inc=cV)
                return cV.n

            for dc in range(16):
                for tg in range(ntg4):
                    s0 = tg * 512
                    rec.emit("T", lambda e, tg=tg, dc=dc, s0=s0: e.matmul(
                        pC[:, s0:s0 + 512],
                        wqb[0][:, dc * 128:dc * 128 + 128],
                        xnt[:, dc * T_CORE + s0:dc * T_CORE + s0 + 512],
                        start=(dc == 0), stop=(dc == 15)),
                        waits=[(sg, tabs), (swo, swo.n), (cV, xn_mark)]
                        if (tg == 0 and dc == 0) else [],
                        inc=cT if (dc == 15 and tg == ntg4 - 1) else None)
            kmm = cT.n
            rec.emit("V", lambda e: e.tensor_scalar_add(
                kbst, pC[:, 0:T_CORE], r5[:, BK:BK + 1]),
                waits=[(cT, kmm), (cV, cV.n)], inc=cV)
            kv_evac = cV.n
            krot = rotary(klocT, kbst, cosk, sink, [(cV, kv_evac)])
            dma("P", cc_ik[:, :], klocT, sg, waits=[(cV, krot)])
            kout = sg.n
            dma("P", wqb[1].rearrange("p (dc o) -> p dc o", o=128),
                wv.ap().rearrange("(dc p) o -> p dc o", p=128), swo)
            vw = swo.n
            for dc in range(16):
                for tg in range(ntg4):
                    s0 = tg * 512
                    rec.emit("T", lambda e, tg=tg, dc=dc, s0=s0: e.matmul(
                        pC[:, s0:s0 + 512],
                        wqb[1][:, dc * 128:dc * 128 + 128],
                        xnt[:, dc * T_CORE + s0:dc * T_CORE + s0 + 512],
                        start=(dc == 0), stop=(dc == 15)),
                        waits=[(swo, vw), (cV, kv_evac)]
                        if (tg == 0 and dc == 0) else [],
                        inc=cT if (dc == 15 and tg == ntg4 - 1) else None)
            vmm = cT.n
            rec.emit("V", lambda e: e.tensor_scalar_add(
                vlocT, pC[:, 0:T_CORE], r5[:, BV:BV + 1]),
                waits=[(cT, vmm), (cV, cV.n)], inc=cV)
            vev = cV.n
            ntt = T_CORE // 128
            tp_done = {}
            for tt in range(ntt):
                pb = pAb if tt % 2 == 0 else pBb
                rec.emit("T", lambda e, tt=tt, pb=pb: e.transpose(
                    pb[:, 0:128], vlocT[:, tt * 128:tt * 128 + 128], ident[:, :]),
                    waits=[(cV, vev) if tt < 2 else (cA, tp_done.get(tt - 2, 0))],
                    inc=cT)
                tpm = cT.n
                rec.emit("A", lambda e, tt=tt, pb=pb: e.activation(
                    out=vstg[:, tt * 128:tt * 128 + 128], in_=pb[:, 0:128],
                    func=AF.Copy), waits=[(cT, tpm)], inc=cA)
                tp_done[tt] = cA.n
            dma("P", cc_iv.ap().rearrange("(tt p) d -> p tt d", p=128),
                vstg.rearrange("p (tt d) -> p tt d", d=128), sg,
                waits=[(cA, cA.n)])
            vout = sg.n
            grp = [[0, 1, 2, 3], [4, 5, 6, 7]]
            rec.emit("P", lambda e: e.collective_compute(
                "AllGather", OP.bypass, replica_groups=grp,
                ins=[cc_ik.ap().opt()], outs=[cc_ok.ap().opt()]),
                waits=[(sg, kout)], inc=scc)
            rec.emit("P", lambda e: e.collective_compute(
                "AllGather", OP.bypass, replica_groups=grp,
                ins=[cc_iv.ap().opt()], outs=[cc_ov.ap().opt()]),
                waits=[(sg, vout)], inc=scc)
            cc_done = scc.n
            for r in range(4):
                dma("S", r3[:, KK + r * T_CORE:KK + (r + 1) * T_CORE],
                    cc_ok[r * 128:(r + 1) * 128, :], sw,
                    waits=[(scc, cc_done - 1), (cV, krot)] if r == 0 else [],
                    ng=(r == 0))
            kload = sw.n

            qmm_mark = {}
            prev_qev = cV.n
            nch = KEYS // 512
            kms = {}
            red = {}
            rec.emit("V", lambda e: e.memset(r5[0:1, KAP + 1:KAP + 2], 0.0),
                     waits=[(cV, cV.n)], inc=cV)
            kmax_chain = cV.n
            for h in range(NH):
                b = h % 2
                for ch in ([2 * (h - 8), 2 * (h - 8) + 1]
                           if (h >= 8 and 2 * (h - 8) + 1 < nch) else []):
                    kb = ch % 2
                    rec.emit("A", lambda e, ch=ch, kb=kb: e.activation(
                        out=pstg[:, kb * 512:kb * 512 + 512],
                        in_=r3[:, KK + ch * 512:KK + ch * 512 + 512],
                        func=AF.Square),
                        waits=[(sw, kload), (cT, kms.get(ch - 2, 0))], inc=cA)
                    sqch = cA.n
                    kp = pB if kb == 0 else pA
                    rec.emit("T", lambda e, kb=kb, kp=kp: e.matmul(
                        kp[0:1, 512:1024],
                        onesb[:, 0:1], pstg[:, kb * 512:kb * 512 + 512],
                        start=True, stop=True),
                        waits=[(cA, sqch), (cV, red.get(ch - 2, kmax_chain))],
                        inc=cT)
                    kms[ch] = cT.n
                    rec.emit("V", lambda e, kp=kp: e.reduce_max(
                        r5[0:1, TMPS + 2:TMPS + 3],
                        kp[0:1, 512:1024],
                        axis=mybir.AxisListType.X),
                        waits=[(cT, kms[ch]), (cV, cV.n)], inc=cV)
                    rec.emit("V", lambda e: e.tensor_tensor(
                        r5[0:1, KAP + 1:KAP + 2], r5[0:1, KAP + 1:KAP + 2],
                        r5[0:1, TMPS + 2:TMPS + 3], OP.max),
                        waits=[(cV, cV.n)], inc=cV)
                    red[ch] = cV.n
                dma("P", wqb[b].rearrange("p (dc o) -> p dc o", o=128),
                    wq.ap().rearrange(
                    "(dc p) o -> p dc o", p=128)[:, :, h * 128:h * 128 + 128], swo,
                    waits=[(cT, qmm_mark.get(h - 2, vmm if h < 2 else 0)),
                           (sg, vout if h == 0 else 0)])
                wql = swo.n
                for dc in range(16):
                    for tg in range(ntg4):
                        s0 = tg * 512
                        rec.emit("T", lambda e, b=b, dc=dc, s0=s0: e.matmul(
                            pC[:, s0:s0 + 512],
                            wqb[b][:, dc * 128:dc * 128 + 128],
                            xnt[:, dc * T_CORE + s0:dc * T_CORE + s0 + 512],
                            start=(dc == 0), stop=(dc == 15)),
                            waits=[(swo, wql), (cV, prev_qev)]
                            if (tg == 0 and dc == 0) else [],
                            inc=cT if (dc == 15 and tg == ntg4 - 1) else None)
                qmm_mark[h] = cT.n
                rec.emit("V", lambda e, h=h: e.tensor_scalar_add(
                    kbst, pC[:, 0:T_CORE], r5[:, BQ + h:BQ + h + 1]),
                    waits=[(cT, qmm_mark[h]), (cV, cV.n)], inc=cV)
                qev = cV.n
                prev_qev = qev
                rec.emit("A", lambda e: e.activation(
                    out=r4.bitcast(BF16)[:, 0:T_CORE], in_=kbst, func=AF.Square),
                    waits=[(cV, qev), (cT, cT.n)], inc=cA)
                sqd = cA.n
                vfree = cV.n
                qa = yacc_b[:, h * T_CORE:(h + 1) * T_CORE]
                qrot = rotary(qa, kbst, cosq, sinq, [(cV, qev), (cA, sqd)])
                rec.emit("V", lambda e, h=h: e.memset(
                    r5[0:1, R2M + h:R2M + h + 1], 0.0), waits=[(cV, vfree)], inc=cV)
                rmm = {}
                for j in range(ntg4):
                    p = pA if j % 2 == 0 else pB
                    rec.emit("T", lambda e, j=j, p=p: e.matmul(
                        p[0:1, 0:512], onesb[:, 0:1],
                        r4.bitcast(BF16)[:, j * 512:j * 512 + 512],
                        start=True, stop=True),
                        waits=[(cA, sqd), (cV, rmm.get(j - 2, cV.n))], inc=cT)
                    rj = cT.n
                    rec.emit("V", lambda e, p=p: e.reduce_max(
                        r5[0:1, TMPS:TMPS + 1], p[0:1, 0:512],
                        axis=mybir.AxisListType.X),
                        waits=[(cT, rj), (cV, cV.n)], inc=cV)
                    rec.emit("V", lambda e, h=h: e.tensor_tensor(
                        r5[0:1, R2M + h:R2M + h + 1], r5[0:1, R2M + h:R2M + h + 1],
                        r5[0:1, TMPS:TMPS + 1], OP.max), waits=[(cV, cV.n)], inc=cV)
                    rmm[j] = cV.n
            qphase_done = (cT.n, cA.n, cV.n)

            kw = [(scc, cc_done), (cV, qphase_done[2]), (cT, qphase_done[0])]
            dma("S", r3[:, VV:VV + NKT * 128].rearrange("p (kt d) -> p kt d", d=128),
                cc_ov.ap().rearrange("(kt p) d -> p kt d", p=128), sw, waits=kw)
            kvload = sw.n
            rec.emit("V", lambda e: e.memset(r5[0:1, KAP:KAP + 1], 0.0), inc=cV)
            rec.emit("A", lambda e: e.activation(
                out=r5[0:1, KAP:KAP + 1], in_=r5[0:1, KAP + 1:KAP + 2],
                func=AF.Sqrt), waits=[(cV, cV.n)], inc=cA)
            kapd = cA.n
            rec.emit("A", lambda e: e.activation(
                out=r5[0:1, TMPS:TMPS + 16], in_=r5[0:1, R2M:R2M + 16],
                func=AF.Sqrt, scale=1.0 / HD), waits=[(cA, kapd), (cV, cV.n)], inc=cA)
            rec.emit("V", lambda e: e.tensor_scalar_mul(
                r5[0:1, TMPS:TMPS + 16], r5[0:1, TMPS:TMPS + 16],
                r5[0:1, KAP:KAP + 1]), waits=[(cA, cA.n)], inc=cV)
            rec.emit("T", lambda e: e.matmul(
                pA[:, 0:16], onesrow[0:1, 0:128], r5[0:1, TMPS:TMPS + 16],
                start=True, stop=True), waits=[(cV, cV.n)], inc=cT)
            rec.emit("V", lambda e: e.tensor_scalar_mul(
                r5[:, MHN:MHN + 16], pA[:, 0:16], -1.0),
                waits=[(cT, cT.n)], inc=cV)
            mh_done = cV.n

            u = T_CORE // 2
            lacc8 = xnt[:, 0:T_CORE]
            lacc32 = xnt_f[:, u:u + T_CORE]
            linv = xnt_f[0:1, u + T_CORE:u + 2 * T_CORE]
            ctxs = xnt[:, 5 * T_CORE:6 * T_CORE]
            pring = [r3[:, PRING + i * QH:PRING + (i + 1) * QH] for i in range(8)]
            ctx_evac = {-1: (cA.n, cV.n)}
            prev_exp63 = {}
            prev_tail = {}
            for h in range(NH):
                qa = yacc_b[:, h * T_CORE:(h + 1) * T_CORE]
                exp_mark = {}
                pv_mark = {}
                l_mark = {}
                for qh in range(2):
                    ps = [pA, pB]
                    cslice = pC[:, qh * QH:qh * QH + QH]
                    for kt in range(NKT):
                        tw = []
                        if kt == 0 and qh == 0:
                            tw += [(sw, kvload), (cV, mh_done)]
                        if h > 0 and kt < 2 and qh == 0:
                            tw.append((cA, prev_exp63[(NKT - 2 + kt, qh)]))
                        if kt < 2 and qh == 1:
                            tw.append((cA, exp_mark[(NKT - 2 + kt, 0)]))
                        if kt >= 2:
                            tw.append((cA, exp_mark[(kt - 2, qh)]))
                        p = ps[kt % 2]
                        for j in range(QH // 512):
                            rec.emit("T", lambda e, kt=kt, j=j, p=p, qa=qa, qh=qh: e.matmul(
                                p[:, j * 512:j * 512 + 512],
                                r3[:, KK + kt * 128:KK + kt * 128 + 128],
                                qa[:, qh * QH + j * 512:qh * QH + j * 512 + 512],
                                start=True, stop=True),
                                waits=tw if j == 0 else [],
                                inc=cT if j == (QH // 512 - 1) else None)
                        sc = cT.n
                        if kt > 0:
                            slot = ((kt - 1) % 4) * 2 + qh
                            pvw = [(cA, exp_mark[(kt - 1, qh)])]
                            if kt == 1 and qh == 0:
                                pvw.append((cV, ctx_evac[h - 1][1]))
                            for j in range(QH // 512):
                                rec.emit("T", lambda e, kt=kt, j=j, slot=slot, cslice=cslice: e.matmul(
                                    cslice[:, j * 512:j * 512 + 512],
                                    r3[:, VV + (kt - 1) * 128:VV + (kt - 1) * 128 + 128],
                                    pring[slot][:, j * 512:j * 512 + 512],
                                    start=(kt - 1 == 0), stop=False),
                                    waits=pvw if j == 0 else [],
                                    inc=cT if j == (QH // 512 - 1) else None)
                            pv_mark[(kt - 1, qh)] = cT.n
                        slot = (kt % 4) * 2 + qh
                        ew = [(cT, sc), (cT, pv_mark.get((kt - 4, qh), 0)),
                              (cV, l_mark.get((kt - 4, qh), 0))]
                        if kt < 4 and h > 0:
                            ew += [(cT, prev_tail[(kt, qh)][0]),
                                   (cV, prev_tail[(kt, qh)][1])]
                        rec.emit("A", lambda e, h=h, p=p, slot=slot: e.activation(
                            out=pring[slot][:, 0:QH], in_=p[:, 0:QH], func=AF.Exp,
                            bias=r5[:, MHN + h:MHN + h + 1]),
                            waits=ew, inc=cA)
                        exp_mark[(kt, qh)] = cA.n
                        lw = [(cA, exp_mark[(kt, qh)])]
                        l8 = lacc8[:, qh * QH:qh * QH + QH]
                        if kt % 8 == 0:
                            rec.emit("V", lambda e, slot=slot, l8=l8: e.tensor_copy(
                                l8, pring[slot][:, 0:QH]),
                                waits=lw + [(cV, cV.n)], inc=cV)
                        else:
                            rec.emit("V", lambda e, slot=slot, l8=l8: e.tensor_tensor(
                                l8, l8, pring[slot][:, 0:QH], OP.add),
                                waits=lw + [(cV, cV.n)], inc=cV)
                        if kt % 8 == 7 or kt == NKT - 1:
                            l32 = lacc32[:, qh * QH:qh * QH + QH]
                            if kt < 8:
                                rec.emit("V", lambda e, l32=l32, l8=l8: e.tensor_copy(
                                    l32, l8), waits=[(cV, cV.n)], inc=cV)
                            else:
                                rec.emit("V", lambda e, l32=l32, l8=l8: e.tensor_tensor(
                                    l32, l32, l8, OP.add), waits=[(cV, cV.n)], inc=cV)
                        l_mark[(kt, qh)] = cV.n
                    slot = ((NKT - 1) % 4) * 2 + qh
                    for j in range(QH // 512):
                        rec.emit("T", lambda e, j=j, slot=slot, cslice=cslice: e.matmul(
                            cslice[:, j * 512:j * 512 + 512],
                            r3[:, VV + (NKT - 1) * 128:VV + (NKT - 1) * 128 + 128],
                            pring[slot][:, j * 512:j * 512 + 512],
                            start=(NKT == 1), stop=True),
                            waits=[(cA, exp_mark[(NKT - 1, qh)])] if j == 0 else [],
                            inc=cT if j == (QH // 512 - 1) else None)
                    pv_mark[(NKT - 1, qh)] = cT.n
                pv_all = cT.n
                lfold = cV.n
                for j in range(T_CORE // 512):
                    rec.emit("A", lambda e, j=j: e.activation(
                        out=ctxs[:, j * 512:j * 512 + 512],
                        in_=pC[:, j * 512:j * 512 + 512], func=AF.Copy),
                        waits=[(cT, pv_all)] if j == 0 else [], inc=cA)
                    cevj = cA.n
                    rec.emit("T", lambda e, j=j: e.matmul(
                        pC[0:1, j * 512:j * 512 + 512], onesf[:, 0:1],
                        lacc32[:, j * 512:j * 512 + 512], start=True, stop=True),
                        waits=[(cA, cevj)] + ([(cV, lfold)] if j == 0 else []),
                        inc=cT)
                    lrj = cT.n
                    rec.emit("V", lambda e, j=j: e.reciprocal(
                        linv[0:1, j * 512:j * 512 + 512],
                        pC[0:1, j * 512:j * 512 + 512]),
                        waits=[(cT, lrj), (cV, cV.n)], inc=cV)
                    rcpj = cV.n
                    rec.emit("T", lambda e, j=j: e.matmul(
                        pC[:, j * 512:j * 512 + 512], onesrow[0:1, 0:128],
                        linv[0:1, j * 512:j * 512 + 512], start=True, stop=True),
                        waits=[(cV, rcpj)], inc=cT)
                    lbj = cT.n
                    rec.emit("V", lambda e, j=j, qa=qa: e.tensor_tensor(
                        qa[:, j * 512:j * 512 + 512], ctxs[:, j * 512:j * 512 + 512],
                        pC[:, j * 512:j * 512 + 512], OP.mult),
                        waits=[(cT, lbj), (cV, cV.n)], inc=cV)
                ctx_evac[h] = (cA.n, cV.n)
                prev_exp63 = dict(exp_mark)
                prev_tail = {(kt, qh): (pv_mark[(NKT - 4 + kt, qh)],
                                        l_mark[(NKT - 4 + kt, qh)])
                             for kt in range(4) for qh in range(2)}

            wob = [r3[:, WOB:WOB + 2048], r3[:, WOB + 2048:WOB + 4096]]
            u = T_CORE // 2
            xtC = [r4[:, 0:T_CORE], xnt_f[:, u + 3 * T_CORE:u + 4 * T_CORE]]
            o_f = xnt_f[:, u + 2 * T_CORE:u + 3 * T_CORE]
            ommT = {}
            ommV = {}
            ostore = {}
            for mo in range(NH):
                b = mo % 2
                dma("P", wob[b].rearrange("p (dc o) -> p dc o", o=128),
                    wo.ap().rearrange(
                    "(dc p) o -> p dc o", p=128)[:, :, mo * 128:mo * 128 + 128], swo,
                    waits=[(cT, ommT.get(mo - 2, 0)),
                           (cV, ctx_evac[NH - 1][1] if mo == 0 else 0)])
                wol = swo.n
                dma("S", xtC[b], xs[mo * 128:(mo + 1) * 128, 0:T_CORE], sw,
                    waits=[(st, ostore.get(mo - 2, 0))])
                xld = sw.n
                tw = [(swo, wol), (cV, ommV.get(mo - 1, cV.n if mo == 0 else 0))]
                for hc in range(16):
                    for j in range(T_CORE // 512):
                        rec.emit("T", lambda e, b=b, j=j, hc=hc: e.matmul(
                            pC[:, j * 512:j * 512 + 512],
                            wob[b][:, hc * 128:hc * 128 + 128],
                            yacc_b[:, hc * T_CORE + j * 512:hc * T_CORE + j * 512 + 512],
                            start=(hc == 0), stop=(hc == 15)),
                            waits=tw if (j == 0 and hc == 0) else [],
                            inc=cT if (j == T_CORE // 512 - 1 and hc == 15) else None)
                omm = cT.n
                ommT[mo] = omm
                rec.emit("V", lambda e, mo=mo: e.tensor_scalar_add(
                    o_f, pC[:, 0:T_CORE], r5[:, BO + mo:BO + mo + 1]),
                    waits=[(cT, omm), (cV, cV.n)], inc=cV)
                ommV[mo] = cV.n
                rec.emit("V", lambda e, b=b: e.tensor_tensor(
                    xtC[b], xtC[b], o_f, OP.add),
                    waits=[(sw, xld), (cV, ommV[mo])], inc=cV)
                dma("A", xs[mo * 128:(mo + 1) * 128, :], xtC[b], st,
                    waits=[(cV, cV.n)])
                ostore[mo] = st.n
            fw = [(st, ostore[fc]) for fc in range(16)]
            xn3a = ln_half(xs, 0, fc_waits=fw, bf16_stage=True,
                           dep_waits=[(cA, ctx_evac[NH - 1][0])])
            xn3b = ln_half(xs, TH, psA=pC[:, 0:1024],
                           psB=pC[:, 1024:2048], fc_waits=fw, bf16_stage=True,
                           dep_waits=[(cT, ommT[NH - 1]), (cV, ommV[NH - 1])])
            return xn3a, xn3b

        init_phase()
        prologue_casts()
        m = ln_half(x_in, 0, dep_waits=[(cV, cV.n), (cP, cP.n)])
        _, xn2a = ffn_half(1, 0, x_in, xs, m, direct=True, fuse_ln=(STAGE >= 2))
        m = ln_half(x_in, TH)
        _, xn2b = ffn_half(1, 1, x_in, xs, m, fuse_ln=(STAGE >= 2),
                           fresh=False)
        if STAGE >= 2:
            xn3a, xn3b = attention(max(xn2a, xn2b))
        else:
            xn3a = xn3b = 0
        if STAGE >= 3:
            ffn_half(3, 0, xs, out_fm, xn3a)
            ffn_half(3, 1, xs, out_fm, xn3b, fresh=False)
        else:
            for fc in range(FC):
                b = fc % 2
                dma("P", r4[:, b * 1024:b * 1024 + 1024],
                    xs[fc * 128:(fc + 1) * 128, 0:1024], sg,
                    waits=[(st, st.n)])
                dma("P", out_fm[fc * 128:(fc + 1) * 128, 0:1024],
                    r4[:, b * 1024:b * 1024 + 1024], sg)
                if T_CORE > 1024:
                    dma("P", r4[:, b * 1024:b * 1024 + 1024],
                        xs[fc * 128:(fc + 1) * 128, 1024:2048], sg)
                    dma("P", out_fm[fc * 128:(fc + 1) * 128, 1024:2048],
                        r4[:, b * 1024:b * 1024 + 1024], sg)

        rec.emit("A", None, waits=[(st, st.n)])
        rec.emit("P", None, waits=[(sg, sg.n), (st, st.n), (spc, spc.n),
                                   (swo, swo.n)])
        rec.emit("S", None, waits=[(sw, sw.n)])

        @block.tensor
        def _(t):
            for f in rec.ops["T"]:
                f(t)

        @block.scalar
        def _(a):
            for f in rec.ops["A"]:
                f(a)

        @block.vector
        def _(v):
            for f in rec.ops["V"]:
                f(v)

        @block.gpsimd
        def _(g):
            for f in rec.ops["P"]:
                f(g)

        @block.sync
        def _(s):
            for f in rec.ops["S"]:
                f(s)

    return nc


def make_core_inputs(inputs, core, T_CORE):
    b = core // 4
    q0 = (core % 4) * T_CORE
    bf = ml_dtypes.bfloat16
    x = np.asarray(inputs["L0_attn_x"], np.float32)
    x_fm = np.ascontiguousarray(x[b, q0:q0 + T_CORE, :].T)
    pos = (q0 + np.arange(T_CORE, dtype=np.float64))[None, :]
    dim = np.arange(64, dtype=np.float64)[:, None]
    ang = pos * (10000.0 ** (-2.0 * dim / 128.0))
    cos = np.cos(ang)
    sin = np.sin(ang)
    s = 1.0 / np.sqrt(128.0)
    cosq = (np.concatenate([cos, cos], 0) * s).astype(np.float32)
    sinq = (np.concatenate([-sin, sin], 0) * s).astype(np.float32)
    cosk = np.concatenate([cos, cos], 0).astype(np.float32)
    sink = np.concatenate([-sin, sin], 0).astype(np.float32)
    d = {
        "x_fm": x_fm,
        "gate_w": np.ascontiguousarray(inputs["L0_ffn_gate_w"], np.int32),
        "up_w": np.ascontiguousarray(inputs["L0_ffn_up_w"], np.int32),
        "down_w": np.ascontiguousarray(inputs["L0_ffn_down_w"], np.int32),
        "wq": np.ascontiguousarray(inputs["L0_attn_Wq"], np.float32),
        "wk": np.ascontiguousarray(inputs["L0_attn_Wk"], np.float32),
        "wv": np.ascontiguousarray(inputs["L0_attn_Wv"], np.float32),
        "wo": np.ascontiguousarray(inputs["L0_attn_Wo"], np.float32),
        "gsc": np.ascontiguousarray(
            np.asarray(inputs["L0_ffn_gate_scale"], np.float32).reshape(64, 128).T),
        "usc": np.ascontiguousarray(
            np.asarray(inputs["L0_ffn_up_scale"], np.float32).reshape(64, 128).T),
        "dsc": np.ascontiguousarray(
            np.asarray(inputs["L0_ffn_down_scale"], np.float32).reshape(16, 128).T),
        "bq": np.ascontiguousarray(
            np.asarray(inputs["L0_attn_bq"], np.float32).reshape(16, 128).T),
        "bk": np.asarray(inputs["L0_attn_bk"], np.float32).reshape(128, 1).copy(),
        "bv": np.asarray(inputs["L0_attn_bv"], np.float32).reshape(128, 1).copy(),
        "bo": np.ascontiguousarray(
            np.asarray(inputs["L0_attn_bo"], np.float32).reshape(16, 128).T),
        "cosq": cosq, "sinq": sinq, "cosk": cosk, "sink": sink,
        "ident": np.eye(128, dtype=bf),
    }
    return d


def kernel(**inputs):
    T_CORE = inputs["L0_attn_x"].shape[1] // 4
    nc = build(T_CORE)
    in_maps = [make_core_inputs(inputs, c, T_CORE) for c in range(8)]
    res = run_bass_kernel_spmd(nc, in_maps, core_ids=list(range(8)))
    B, T = inputs["L0_attn_x"].shape[0], inputs["L0_attn_x"].shape[1]
    out = np.empty((B, T, D), np.float32)
    for c in range(8):
        b = c // 4
        q0 = (c % 4) * T_CORE
        out[b, q0:q0 + T_CORE, :] = np.asarray(res.results[c]["out_fm"]).T
    return out
